# version 02 (final)
# baseline (speedup 1.0000x reference)
import numpy as np
import ml_dtypes
import concourse.bass as bass
import concourse.mybir as mybir
from concourse.bass_utils import run_bass_kernel_spmd
from contextlib import ExitStack

F32 = mybir.dt.float32
BF16 = mybir.dt.bfloat16
AF = mybir.ActivationFunctionType
ALU = mybir.AluOpType
AX = mybir.AxisListType
BF = ml_dtypes.bfloat16

B, S, D = 4, 4096, 1024
NCORE = 8
T = 2048
DC = D // 128
TT = T // 128
EPS = 1e-5

ENGS = ("tensor", "vector", "scalar", "gpsimd", "sync")


class Ctx:

    def __init__(self, nc, stack):
        self.nc = nc
        self.stack = stack
        self.sem = {e: stack.enter_context(nc.semaphore("sem_" + e)) for e in ENGS}
        self.cnt = {e: 0 for e in ENGS}
        self.queue = {e: [] for e in ENGS}
        self.cover = {}
        self.consts = {}
        self.const_vals = []
        self.const_sb = None

    def sbuf(self, name, shape, dtype):
        return self.stack.enter_context(self.nc.sbuf_tensor(name, list(shape), dtype))

    def psum_t(self, name, shape, dtype=F32):
        return self.stack.enter_context(self.nc.psum_tensor(name, list(shape), dtype))

    def dram(self, name, shape, dtype, kind="Internal"):
        return self.nc.dram_tensor(name, list(shape), dtype, kind=kind)

    def const(self, value, dtype=F32):
        key = (dtype, float(value))
        if key not in self.consts:
            idx = len(self.const_vals)
            assert idx < self.const_sb.shape[-1]
            self.consts[key] = idx
            self.const_vals.append(float(value))
            self.nc.const_aps.register(dtype, float(value), self.const_sb[0:1, idx:idx + 1])
        return self.const_sb[0:1, self.consts[key]:self.consts[key] + 1]

    def tok(self, eng):
        return (eng, self.cnt[eng])

    def seal(self):
        return [self.tok(e) for e in ENGS]

    SELFSYNC = ("vector", "scalar")

    def emit(self, eng, fn, deps=(), n_inc=1):
        waits = []
        if eng in self.SELFSYNC and self.cnt[eng] > 0:
            waits.append((self.sem[eng], self.cnt[eng]))
        for (p, v) in deps:
            if v <= 0:
                continue
            if p in ("sync", "gpsimd"):
                v = self.cnt[p]
            if p == eng:
                if eng in ("sync", "gpsimd") and v <= self.cnt[eng]:
                    key = (eng, p)
                    if self.cover.get(key, 0) < v:
                        self.cover[key] = v
                        waits.append((self.sem[p], v))
                continue
            key = (eng, p)
            if self.cover.get(key, 0) < v:
                self.cover[key] = v
                waits.append((self.sem[p], v))
        sem = self.sem[eng]

        def closure(engine, _waits=tuple(waits), _fn=fn, _n=n_inc):
            for (s, v) in _waits:
                engine.wait_ge(s, v)
            last = _fn(engine)
            if _n:
                assert last is not None, "op closure must return its last instruction"
                last.then_inc(sem, _n)

        self.queue[eng].append(closure)
        self.cnt[eng] += n_inc
        return (eng, self.cnt[eng])

    def dma(self, eng, out, in_, deps=(), **kw):
        return self.emit(eng, lambda e: e.dma_start(out=out, in_=in_, **kw),
                         deps=deps, n_inc=16)

    def finalize(self, block):
        fin = [(e, self.cnt[e]) for e in ("sync", "gpsimd") if self.cnt[e] > 0]
        if fin:
            def endwait(engine, _fin=tuple(fin)):
                for e, v in _fin:
                    engine.wait_ge(self.sem[e], v)
                engine.tensor_copy(self.r0[0:1, 0:1], self.r1[0:1, 0:1])
            self.queue["vector"].append(endwait)
        for e in ENGS:
            ops = self.queue[e]
            if not ops:
                continue

            def run(engine, _ops=tuple(ops)):
                for op in _ops:
                    op(engine)

            getattr(block, e)(run)



class Prog:

    def __init__(self, name="mk"):
        self.nc = bass.Bass("TRN2", target_bir_lowering=False, debug=False)
        self.stack = ExitStack()
        self.ctx = Ctx(self.nc, self.stack)
        self.host_inputs = {}

    def inp(self, name, shape, dtype, fn):
        t = self.nc.dram_tensor(name, list(shape), dtype, kind="ExternalInput")
        self.host_inputs[name] = fn
        return t.ap()

    def out(self, name, shape, dtype):
        return self.nc.dram_tensor(name, list(shape), dtype, kind="ExternalOutput").ap()

    def setup_pools(self):
        c = self.ctx
        c.ps = [c.psum_t(f"ps{i}", [128, 512], F32) for i in range(8)]
        c.PF = c.sbuf("PF", [128, 16384], F32)
        c.PA = c.sbuf("PA", [128, 16384], BF16)
        c.PB = c.sbuf("PB", [128, 16384], BF16)
        c.xn = c.sbuf("xn", [128, DC, T], BF16)
        c.r0 = c.sbuf("r0", [1, 2048], F32)
        c.r1 = c.sbuf("r1", [1, 2048], F32)
        c.r2 = c.sbuf("r2", [1, 2048], F32)
        c.tmpf = c.sbuf("tmpf", [128, 512], F32)
        c.tmpf2 = c.sbuf("tmpf2", [128, 512], F32)
        c.smallf = c.sbuf("smallf", [128, 256], F32)
        c.rstd_col = c.sbuf("rstd_col", [128, TT], F32)
        c.qks0 = c.sbuf("qks0", [2, 512], F32)
        c.qks1 = c.sbuf("qks1", [2, 512], F32)
        c.qks2 = c.sbuf("qks2", [2, 512], F32)
        c.hnq = c.sbuf("hnq", [2, 8], F32)
        c.hnk = c.sbuf("hnk", [2, 8], F32)
        c.hnc = c.sbuf("hnc", [2, 8], F32)
        c.bones_col_bf = c.sbuf("bones_col_bf", [128, 2], BF16)
        c.ident_sb = c.sbuf("ident_sb", [128, 128], BF16)
        c.ident_f = c.sbuf("ident_f", [128, 128], F32)
        c.mu_col = c.sbuf("mu_col", [128, TT], F32)
        c.bones_row_f = c.sbuf("bones_row_f", [2, 128], F32)
        self.ncd = c.dram("ncd_scratch", [16], F32)
        c.logits = c.sbuf("logits", [128, TT * 8], F32)
        c.gates = c.sbuf("gates", [128, TT * 8], F32)
        c.const_sb = c.sbuf("constpool", [1, 64], F32)
        c.ones_row_bf = c.sbuf("ones_row_bf", [1, 128], BF16)
        c.ones_row_f = c.sbuf("ones_row_f", [1, 128], F32)
        c.ones_col_bf = c.sbuf("ones_col_bf", [128, 1], BF16)
        c.ones_col_f = c.sbuf("ones_col_f", [128, 1], F32)

    def init_ops(self, consts_name="consts"):
        c = self.ctx
        self.consts_in = self.inp(consts_name, [1, 64], F32,
                                  lambda ci, core: np.array(c.const_vals + [0.0] * (64 - len(c.const_vals)),
                                                            np.float32)[None, :])
        c.dma("sync", c.const_sb[:, :], self.consts_in[0:1, :])
        bcol = self.inp("bones_col", [128, 2], BF16,
                        lambda ci, core: np.asarray([[1.0 if pp < 64 else 0.0, 1.0 if pp >= 64 else 0.0] for pp in range(128)], BF))
        brow = self.inp("bones_row", [2, 128], F32,
                        lambda ci, core: np.asarray([[1.0 if pp < 64 else 0.0 for pp in range(128)], [1.0 if pp >= 64 else 0.0 for pp in range(128)]], np.float32))
        c.dma("sync", c.bones_col_bf[:, :], bcol[:, :])
        c.dma("sync", c.bones_row_f[:, :], brow[:, :])
        ident = self.inp("ident", [128, 128], BF16,
                         lambda ci, core: np.eye(128, dtype=BF))
        c.dma("sync", c.ident_sb[:, :], ident[:, :])
        identf = self.inp("identf", [128, 128], F32,
                          lambda ci, core: np.eye(128, dtype=np.float32))
        c.dma("sync", c.ident_f[:, :], identf[:, :])
        d = c.seal()
        c.emit("vector", lambda v: v.memset(c.ones_row_bf[:, :], 1.0), deps=d)
        c.emit("vector", lambda v: v.memset(c.ones_row_f[:, :], 1.0))
        c.emit("vector", lambda v: v.memset(c.ones_col_bf[:, :], 1.0))
        c.emit("vector", lambda v: v.memset(c.ones_col_f[:, :], 1.0))

    def build(self, fn):
        with self.stack:
            with self.nc.Block() as block:
                self.setup_pools()
                fn(self)
                self.ctx.finalize(block)
        return self.nc

    def run(self, inputs_by_core, **kw):
        in_maps = []
        for cidx in range(NCORE):
            m = {}
            for name, fn in self.host_inputs.items():
                m[name] = np.ascontiguousarray(fn(inputs_by_core[cidx], cidx))
            in_maps.append(m)
        return run_bass_kernel_spmd(self.nc, in_maps, core_ids=list(range(NCORE)), **kw)



def emit_ln(p, x_hbm):
    c = p.ctx
    xf = c.PF[:, :].rearrange("p (c t) -> p c t", c=DC)
    xb = c.PA[:, :].rearrange("p (c t) -> p c t", c=DC)
    sq = c.PB[:, :].rearrange("p (c t) -> p c t", c=DC)
    d0 = c.seal()
    for ch in range(DC):
        c.dma("sync", xf[:, ch, :], x_hbm[ch * 128:(ch + 1) * 128, :], deps=d0)
    dld = c.seal()
    for ch in range(DC):
        c.emit("vector", lambda v, ch=ch: v.tensor_copy(xb[:, ch, :], xf[:, ch, :]), deps=dld)
        c.emit("scalar", lambda s, ch=ch: s.activation(sq[:, ch, :], xf[:, ch, :], AF.Square), deps=dld)
    dsq = c.seal()
    prev_read = None
    for tc in range(4):
        sl = slice(tc * 512, (tc + 1) * 512)

        def mm_stats(t, sl=sl):
            for ch in range(DC):
                t.matmul(c.ps[0][0:1, :], c.ones_col_bf[:, :], xb[:, ch, sl],
                         start=(ch == 0), stop=(ch == DC - 1))
            last = None
            for ch in range(DC):
                last = t.matmul(c.ps[1][0:1, :], c.ones_col_bf[:, :], sq[:, ch, sl],
                                start=(ch == 0), stop=(ch == DC - 1))
            return last
        war = [prev_read] if prev_read else []
        tmm = c.emit("tensor", mm_stats, deps=dsq + war)
        c.emit("vector", lambda v, sl=sl: v.tensor_scalar(c.r0[0:1, sl], c.ps[0][0:1, :], 1.0 / D, None, ALU.mult), deps=[tmm])
        prev_read = c.emit("vector", lambda v, sl=sl: v.tensor_scalar(c.r1[0:1, sl], c.ps[1][0:1, :], 1.0 / D, None, ALU.mult), deps=[tmm])
    fl = slice(0, 2048)
    c.emit("vector", lambda v, fl=fl: v.tensor_tensor(c.r2[0:1, fl], c.r0[0:1, fl], c.r0[0:1, fl], ALU.mult))
    c.emit("vector", lambda v, fl=fl: v.tensor_sub(c.r1[0:1, fl], c.r1[0:1, fl], c.r2[0:1, fl]))
    c.emit("vector", lambda v, fl=fl: v.tensor_scalar(c.r1[0:1, fl], c.r1[0:1, fl], EPS, None, ALU.add))
    dv = c.seal()
    c.emit("scalar", lambda s, fl=fl: s.activation(c.r2[0:1, fl], c.r1[0:1, fl], AF.Sqrt), deps=dv)
    dvs = c.seal()
    c.emit("vector", lambda v, fl=fl: v.reciprocal(c.r1[0:1, fl], c.r2[0:1, fl]), deps=dvs)
    dmr = c.seal()
    prev_read = None
    for tc in range(4):
        sl = slice(tc * 512, (tc + 1) * 512)

        def mm_bcast(t, sl=sl):
            t.matmul(c.ps[2][:, :], c.ones_row_f[0:1, :], c.r0[0:1, sl], start=True, stop=True)
            return t.matmul(c.ps[3][:, :], c.ones_row_f[0:1, :], c.r1[0:1, sl], start=True, stop=True)
        war = [prev_read] if prev_read else []
        tb = c.emit("tensor", mm_bcast, deps=dmr + war)
        for ch in range(DC):
            c.emit("vector", lambda v, ch=ch, sl=sl: v.tensor_sub(c.tmpf[:, :], xf[:, ch, sl], c.ps[2][:, :]), deps=[tb])
            prev_read = c.emit("vector", lambda v, ch=ch, sl=sl: v.tensor_mul(c.xn[:, ch, sl], c.tmpf[:, :], c.ps[3][:, :]))
    return c.seal()


def emit_combine(p, x_in, x_out, s_dram, g_tm):
    c = p.ctx
    sc = c.PB.ap()[:, 0:512]
    gf = c.PB.ap()[:, 512:1024]
    bk = Banks(c, range(0, 4))
    d0 = c.seal()
    prev = {}
    for ch in range(DC):
        for tc in range(4):
            sl = slice(tc * 512, (tc + 1) * 512)
            war = [prev[k] for k in prev]
            ldx = c.dma("sync", c.tmpf[:, :], x_in[ch * 128:(ch + 1) * 128, sl], deps=d0 + war)
            lds = c.dma("sync", sc, s_dram.ap()[ch * 128:(ch + 1) * 128, sl], deps=d0 + war)
            tev = []
            for j in range(4):
                tt = tc * 4 + j
                b, warb = bk.take()
                tr = c.emit("tensor", lambda t, tt=tt, ch=ch, b=b: t.transpose(
                    c.ps[b].ap().bitcast(BF16)[:, 0:128], g_tm[:, tt, ch * 128:(ch + 1) * 128],
                    c.ident_sb[:, :]), deps=warb + d0)
                ev = c.emit("vector", lambda v, j=j, b=b: v.tensor_copy(
                    gf[:, j * 128:(j + 1) * 128], c.ps[b].ap().bitcast(BF16)[:, 0:128]), deps=[tr])
                bk.took(b, ev)
                tev.append(ev)
            m = c.emit("vector", lambda v: v.tensor_mul(c.tmpf2[:, :], sc, gf), deps=[lds] + tev)
            a = c.emit("vector", lambda v: v.tensor_add(c.tmpf[:, :], c.tmpf[:, :], c.tmpf2[:, :]), deps=[ldx])
            st = c.dma("sync", x_out[ch * 128:(ch + 1) * 128, sl], c.tmpf[:, :], deps=[a])
            prev = {"st": st, "m": m}
    return c.seal()


class Banks:

    def __init__(self, ctx, ids):
        self.ctx = ctx
        self.ids = list(ids)
        self.state = {i: None for i in self.ids}
        self.rr = 0

    def take(self):
        i = self.ids[self.rr % len(self.ids)]
        self.rr += 1
        war = self.state[i]
        return i, ([war] if war else [])

    def took(self, i, read_tok):
        self.state[i] = read_tok


def emit_row_to_col(p, row_ap, out_sb, bank, ncols, deps):
    c = p.ctx

    def mm(t):
        last = None
        for i in range(ncols):
            last = t.matmul(c.ps[bank][:, i:i + 1], row_ap[0:1, i * 128:(i + 1) * 128],
                            c.ones_row_f[0:1, 0:1], start=True, stop=True)
        return last
    tm = c.emit("tensor", mm, deps=deps)
    ev = c.emit("vector", lambda v: v.tensor_copy(out_sb[:, 0:ncols], c.ps[bank][:, 0:ncols]), deps=[tm])
    return ev


def emit_router_f32(p, xf, rw_ap, rb_ap, nlog, logits_sb, rstd_col, extra_sb):
    c = p.ctx
    rwf = c.sbuf(f"rw_{rw_ap.name}", [128, DC, nlog], F32)
    rbf = c.sbuf(f"rb_{rb_ap.name}", [1, nlog], F32)
    d0 = c.seal()
    for ch in range(DC):
        c.dma("sync", rwf[:, ch, :], rw_ap[ch * 128:(ch + 1) * 128, :], deps=d0)
    c.dma("sync", rbf[0:1, :], rb_ap[0:1, :], deps=d0)
    dw = c.seal()
    csn = extra_sb[:, 0:nlog]
    rbb = extra_sb[:, nlog:2 * nlog]

    def mm_cs(t):
        for ch in range(DC):
            t.matmul(c.ps[0][0:1, 0:nlog], c.ones_col_f[:, :], rwf[:, ch, :],
                     start=(ch == 0), stop=(ch == DC - 1))
        return t.matmul(c.ps[1][:, 0:nlog], c.ones_row_f[0:1, :], rbf[0:1, :], start=True, stop=True)
    tcs = c.emit("tensor", mm_cs, deps=dw)
    e1 = c.emit("vector", lambda v: v.tensor_scalar(csn[0:1, :], c.ps[0][0:1, 0:nlog], -1.0, None, ALU.mult), deps=[tcs])
    e2 = c.emit("vector", lambda v: v.tensor_copy(rbb[:, :], c.ps[1][:, 0:nlog]))
    bk = Banks(c, range(2, 6))
    for tt in range(TT):
        tsl = slice(tt * 128, (tt + 1) * 128)
        b, war = bk.take()

        def mm(t, tsl=tsl, b=b, tt=tt):
            for ch in range(DC):
                t.matmul(c.ps[b][:, 0:nlog], xf[:, ch, tsl], rwf[:, ch, :],
                         start=(ch == 0), stop=False)
            return t.matmul(c.ps[b][:, 0:nlog], c.r0[0:1, tsl], csn[0:1, :], start=False, stop=True)
        tm = c.emit("tensor", mm, deps=dw + war + [e1, e2])
        ev = c.emit("vector", lambda v, b=b, tt=tt: v.scalar_tensor_tensor(
            logits_sb[:, tt * nlog:(tt + 1) * nlog], c.ps[b][:, 0:nlog],
            rstd_col[:, tt:tt + 1], rbb[:, :], ALU.mult, ALU.add), deps=[tm])
        bk.took(b, ev)
    return c.seal()


def emit_topk2_gates(p, logits_sb, gates_sb, scr, ne=8, mode="l0"):
    c = p.ctx
    n = TT * ne
    L = logits_sb[:, 0:n].rearrange("p (t e) -> p t e", t=TT)
    S1 = scr[:, 0:n].rearrange("p (t e) -> p t e", t=TT)
    S2 = scr[:, n:2 * n].rearrange("p (t e) -> p t e", t=TT)
    m1 = scr[:, 2 * n:2 * n + TT]
    m2 = scr[:, 2 * n + TT:2 * n + 2 * TT]
    t1 = scr[:, 2 * n + 2 * TT:2 * n + 3 * TT]
    t2 = scr[:, 2 * n + 3 * TT:2 * n + 4 * TT]
    G = gates_sb[:, 0:n].rearrange("p (t e) -> p t e", t=TT)
    d = c.seal()

    def b(ap):
        return ap.unsqueeze(2).broadcast_to([128, TT, ne])
    c.emit("vector", lambda v: v.reduce_max(m1, L, axis=AX.X), deps=d)
    c.emit("vector", lambda v: v.tensor_tensor(S1, L, b(m1), ALU.subtract))
    c.emit("vector", lambda v: v.tensor_scalar(S2, S1, 0.0, -1e30, ALU.is_ge, ALU.mult))
    c.emit("vector", lambda v: v.tensor_tensor(S2, S2, L, ALU.add))
    c.emit("vector", lambda v: v.reduce_max(m2, S2, axis=AX.X))
    dv = c.seal()
    c.emit("scalar", lambda s: s.activation(S1, S1, AF.Exp), deps=dv)
    dv2 = c.seal()
    c.emit("vector", lambda v: v.tensor_sub(t1, m2, m1), deps=dv2)
    c.emit("scalar", lambda s: s.activation(t1, t1, AF.Exp), deps=c.seal())
    c.emit("vector", lambda v: v.tensor_scalar(t1, t1, 1.0, None, ALU.add), deps=c.seal())
    c.emit("vector", lambda v: v.reciprocal(t2, t1))
    c.emit("vector", lambda v: v.tensor_tensor(S2, L, b(m2), ALU.is_ge))
    c.emit("vector", lambda v: v.tensor_mul(S1, S1, S2))
    c.emit("vector", lambda v: v.tensor_tensor(G, S1, b(t2), ALU.mult))
    return c.seal()


def moe_ffn(p, xn, w1_ap, w3_ap, w2_ap, gates_sb, y_acc, NE=10, NROUTED=8, H=512):
    c = p.ctx
    HC = H // 128
    PBv = c.PB

    def w13(buf, w, ch):
        off = ((buf * 2 + w) * DC + ch) * 128
        return PBv[:, off:off + 128]

    def w13b(buf, w):
        off = (buf * 2 + w) * DC * 128
        return PBv[:, off:off + DC * 128].rearrange("p (c h) -> p c h", c=DC)

    W2B = 2 if HC <= 4 else 1

    def w2b(buf, hc):
        off = 4096 + (buf * HC + hc) * 1024
        return PBv[:, off:off + 1024]

    def w2full(buf):
        off = 4096 + buf * HC * 1024
        return PBv[:, off:off + HC * 1024].rearrange("p (h d) -> p h d", h=HC)
    h_sb = c.PA.ap()[:, 0:HC * T].rearrange("p (h t) -> p h t", h=HC)
    rd_tok = {}
    w2_ld, w2_rd = {}, {}
    ybk = Banks(c, range(4, 8))
    hbk = Banks(c, range(0, 4))
    dep0 = c.seal()
    prev_m = None
    h_war = None
    for e in range(NE):
        wb2 = e % W2B
        w2war = [w2_rd[wb2]] if wb2 in w2_rd else dep0
        w2_ld[wb2] = c.dma("gpsimd", w2full(wb2), w2_ap[e].rearrange("(h p) d -> p h d", p=128), deps=w2war)
        for hc in range(HC):
            buf = (e * HC + hc) % 2
            war = [rd_tok[buf]] if buf in rd_tok else dep0
            l1 = c.dma("gpsimd", w13b(buf, 0), w1_ap[e, :, hc * 128:(hc + 1) * 128].rearrange("(c p) h -> p c h", p=128), deps=war)
            l2 = c.dma("gpsimd", w13b(buf, 1), w3_ap[e, :, hc * 128:(hc + 1) * 128].rearrange("(c p) h -> p c h", p=128), deps=war)
            for tc in range(4):
                tsl = slice(tc * 512, (tc + 1) * 512)
                b1, war1 = hbk.take()
                b3, war3 = hbk.take()

                def mmh(t, tsl=tsl, b1=b1, b3=b3, buf=buf):
                    for ch in range(DC):
                        t.matmul(c.ps[b1][:, :], w13(buf, 0, ch), xn[:, ch, tsl],
                                 start=(ch == 0), stop=(ch == DC - 1))
                    last = None
                    for ch in range(DC):
                        last = t.matmul(c.ps[b3][:, :], w13(buf, 1, ch), xn[:, ch, tsl],
                                        start=(ch == 0), stop=(ch == DC - 1))
                    return last
                tm = c.emit("tensor", mmh, deps=[l1, l2] + war1 + war3)
                rd_tok[buf] = tm
                aw = [tm] + ([prev_m] if prev_m else [])
                a = c.emit("scalar", lambda s, b1=b1: s.activation(c.tmpf[:, :], c.ps[b1][:, :], AF.Silu), deps=aw)
                mw = [a] + ([h_war] if h_war else [])
                m = c.emit("vector", lambda v, b3=b3, hc=hc, tsl=tsl: v.tensor_mul(h_sb[:, hc, tsl], c.tmpf[:, :], c.ps[b3][:, :]), deps=mw)
                prev_m = m
                hbk.took(b1, m)
                hbk.took(b3, m)
        dh = c.seal()
        for tt in range(TT):
            tsl = slice(tt * 128, (tt + 1) * 128)
            b, war = ybk.take()
            b2, war2 = ybk.take()

            def mmy2(t, tsl=tsl, b=b, b2=b2, wb2=wb2):
                last = None
                for half, bb in ((0, b), (1, b2)):
                    for hc in range(HC):
                        last = t.matmul(c.ps[bb][:, :], h_sb[:, hc, tsl],
                                        w2b(wb2, hc)[:, half * 512:(half + 1) * 512],
                                        start=(hc == 0), stop=(hc == HC - 1))
                return last
            tm = c.emit("tensor", mmy2, deps=[w2_ld[wb2]] + war + war2 + dh)
            h_war = tm
            g_ap = gates_sb[:, tt * 8 + e:tt * 8 + e + 1] if e < NROUTED else None
            for half, bb in ((0, b), (1, b2)):
                ya = y_acc[:, tt, half * 512:(half + 1) * 512]
                if e == 0:
                    r = c.emit("vector", lambda v, ya=ya, bb=bb, g_ap=g_ap: v.tensor_scalar(ya, c.ps[bb][:, :], g_ap, None, ALU.mult), deps=[tm])
                elif g_ap is not None:
                    r = c.emit("vector", lambda v, ya=ya, bb=bb, g_ap=g_ap: v.scalar_tensor_tensor(ya, c.ps[bb][:, :], g_ap, ya, ALU.mult, ALU.add), deps=[tm])
                else:
                    r = c.emit("vector", lambda v, ya=ya, bb=bb: v.tensor_add(ya, c.ps[bb][:, :], ya), deps=[tm])
            ybk.took(b, r)
            ybk.took(b2, r)
            w2_rd[wb2] = r
    return c.seal()


def emit_silu_tm(p, y_acc, g_tm):
    c = p.ctx
    d = c.seal()
    for tt in range(TT):
        for half in range(2):
            c.emit("scalar", lambda s, tt=tt, half=half: s.activation(
                g_tm[:, tt, half * 512:(half + 1) * 512],
                y_acc[:, tt, half * 512:(half + 1) * 512], AF.Silu), deps=d)
    return c.seal()


def emit_qk_ln(p, q_sb, ncol_chunks, tlen, gamma_sb, beta_sb, sqscr):
    c = p.ctx
    nt = tlen // 512
    bk = Banks(c, range(0, 4))
    dqk0 = c.seal()
    for ch in range(ncol_chunks):
        for tc in range(nt):
            sl = slice(tc * 512, (tc + 1) * 512)
            sq = c.emit("scalar", lambda s, ch=ch, sl=sl: s.activation(sqscr[:, 0:512], q_sb[:, ch, sl], AF.Square), deps=dqk0)
            b0, war0 = bk.take()
            b1, war1 = bk.take()

            def mm(t, ch=ch, sl=sl, b0=b0, b1=b1):
                t.matmul(c.ps[b0][0:2, :], c.bones_col_bf[:, :], q_sb[:, ch, sl], start=True, stop=True)
                return t.matmul(c.ps[b1][0:2, :], c.bones_col_bf[:, :], sqscr[:, 0:512], start=True, stop=True)
            tm = c.emit("tensor", mm, deps=[sq] + war0 + war1)
            e1 = c.emit("vector", lambda v, b0=b0: v.tensor_scalar(c.qks0[:, :], c.ps[b0][0:2, :], 1.0 / 64, None, ALU.mult), deps=[tm])
            e2 = c.emit("vector", lambda v, b1=b1: v.tensor_scalar(c.qks1[:, :], c.ps[b1][0:2, :], 1.0 / 64, None, ALU.mult), deps=[tm])
            bk.took(b0, e1)
            bk.took(b1, e2)
            c.emit("vector", lambda v: v.tensor_tensor(c.qks2[:, :], c.qks0[:, :], c.qks0[:, :], ALU.mult))
            c.emit("vector", lambda v: v.tensor_sub(c.qks1[:, :], c.qks1[:, :], c.qks2[:, :]))
            c.emit("vector", lambda v: v.tensor_scalar(c.qks1[:, :], c.qks1[:, :], EPS, None, ALU.add))
            dsq = c.seal()
            c.emit("scalar", lambda s: s.activation(c.qks2[:, :], c.qks1[:, :], AF.Sqrt), deps=dsq)
            dsq2 = c.seal()
            c.emit("vector", lambda v: v.reciprocal(c.qks1[:, :], c.qks2[:, :]), deps=dsq2)
            dmr = c.seal()
            b2, war2 = bk.take()
            b3, war3 = bk.take()

            def mmb(t, b2=b2, b3=b3):
                t.matmul(c.ps[b2][:, :], c.bones_row_f[:, :], c.qks0[:, :], start=True, stop=True)
                return t.matmul(c.ps[b3][:, :], c.bones_row_f[:, :], c.qks1[:, :], start=True, stop=True)
            tb = c.emit("tensor", mmb, deps=dmr + war2 + war3)
            c.emit("vector", lambda v, ch=ch, sl=sl, b2=b2: v.tensor_sub(c.tmpf[:, :], q_sb[:, ch, sl], c.ps[b2][:, :]), deps=[tb])
            e3 = c.emit("vector", lambda v, b3=b3: v.tensor_mul(c.tmpf[:, :], c.tmpf[:, :], c.ps[b3][:, :]))
            e4 = c.emit("vector", lambda v, ch=ch, sl=sl: v.tensor_scalar(q_sb[:, ch, sl], c.tmpf[:, :], gamma_sb, beta_sb, ALU.mult, ALU.add))
            bk.took(b2, e3)
            bk.took(b3, e4)
    return c.seal()


def emit_head_maxnorm(p, q_sb, ncol_chunks, tlen, out_row, sqscr):
    c = p.ctx
    nt = tlen // 512
    bk = Banks(c, range(4, 6))
    dmn0 = c.seal()
    for ch in range(ncol_chunks):
        for tc in range(nt):
            sl = slice(tc * 512, (tc + 1) * 512)
            sqt = c.emit("scalar", lambda s, ch=ch, sl=sl: s.activation(sqscr[:, 0:512], q_sb[:, ch, sl], AF.Square), deps=dmn0)
            b2, war2 = bk.take()
            tm = c.emit("tensor", lambda t, b2=b2: t.matmul(c.ps[b2][0:2, :], c.bones_col_bf[:, :], sqscr[:, 0:512], start=True, stop=True), deps=[sqt] + war2)
            if tc == 0:
                ev = c.emit("vector", lambda v, b2=b2, ch=ch: v.reduce_max(out_row[0:2, ch:ch + 1], c.ps[b2][0:2, :], axis=AX.X), deps=[tm])
            else:
                ev = c.emit("vector", lambda v, b2=b2: v.reduce_max(c.qks2[:, 0:1], c.ps[b2][0:2, :], axis=AX.X), deps=[tm])
                ev = c.emit("vector", lambda v, ch=ch: v.tensor_max(out_row[0:2, ch:ch + 1], out_row[0:2, ch:ch + 1], c.qks2[:, 0:1]))
            bk.took(b2, ev)
    return c.seal()


def l0_attn(p, xn, Wqkv_ap, bqk_ap, bv_ap, qgam, qbet, kgam, kbet, Wout_ap, bout_ap,
            hflag_ap, m0_ap, m4_ap, halo_in, halo_ag, s_dram, o_dram):
    c = p.ctx
    FB = c.PF.ap().bitcast(BF16)
    q_sb = FB[:, 0:8192].rearrange("p (c t) -> p c t", c=8)
    k_sb = FB[:, 8192:20480].rearrange("p (c t) -> p c t", c=8)
    ptbuf = FB[:, 20480:24576].rearrange("p (b t) -> p b t", b=8)
    wq_sb = FB[:, 24576:26624].rearrange("p (b c h) -> p b c h", b=2, c=DC)
    wv_sb = FB[:, 26624:30720].rearrange("p (c h) -> p c h", c=DC)
    m0_sb = FB[:, 30720:30848]
    m4_sb = FB[:, 30848:30976]
    brow = FB[:, 30976:31488]
    hx = c.PA.ap()[:, 12288:16384].rearrange("p (c t) -> p c t", c=8)
    sqscr = c.PA.ap()[:, 11264:11776]
    v_sb = c.PB.ap()[:, 0:12480].rearrange("p (k h e) -> p k h e", k=12, h=16)
    bqk = c.smallf[:, 0:16]
    bout = c.smallf[:, 16:24]
    hfl = c.smallf[:, 24:25]
    cbias = c.smallf[:, 32:48]
    ncd = p.ncd
    d0 = c.seal()
    c.dma("sync", m0_sb[:, :], m0_ap[:, :], deps=d0)
    c.dma("sync", m4_sb[:, :], m4_ap[:, :], deps=d0)
    c.dma("sync", bqk[:, :], bqk_ap[:, :], deps=d0)
    c.dma("sync", bout[:, :], bout_ap[:, :], deps=d0)
    c.dma("sync", hfl[:, :], hflag_ap[:, :], deps=d0)
    for ch in range(DC):
        c.dma("sync", halo_in.ap()[ch * 128:(ch + 1) * 128, :], xn[:, ch, 1536:2048], deps=d0)
    dhi = c.seal()
    cc = c.emit("gpsimd", lambda g: g.collective_compute(
        "AllGather", ALU.bypass, replica_groups=[[0, 1], [2, 3], [4, 5], [6, 7]],
        ins=[halo_in.ap()[:, :]], outs=[halo_ag.ap()[:, :]]), deps=dhi, n_inc=1)
    for ch in range(DC):
        c.dma("sync", hx[:, ch, :], halo_ag.ap()[ch * 128:(ch + 1) * 128, :], deps=[cc])

    def proj_fm(dst, ocol0, nchunks, rhs_srcs, bias_col, dep):
        bk = Banks(c, range(0, 4))
        wld, wrd = {}, {}
        for oc in range(nchunks):
            buf = oc % 2
            war = [wrd[buf]] if buf in wrd else dep
            osl = slice((ocol0 + oc) * 128, (ocol0 + oc + 1) * 128)
            wld[buf] = c.dma("gpsimd", wq_sb[:, buf], Wqkv_ap[:, osl].rearrange("(c p) h -> p c h", p=128), deps=war)
            for gi, src in enumerate(rhs_srcs):
                b, warb = bk.take()

                def mm(t, buf=buf, src=src, b=b):
                    last = None
                    for ch in range(DC):
                        last = t.matmul(c.ps[b][:, :], wq_sb[:, buf, ch], src[ch],
                                        start=(ch == 0), stop=(ch == DC - 1))
                    return last
                tm = c.emit("tensor", mm, deps=[wld[buf]] + warb + dep)
                wrd[buf] = tm
                ev = c.emit("scalar", lambda s, oc=oc, gi=gi, b=b: s.activation(
                    dst[:, oc, gi * 512:(gi + 1) * 512], c.ps[b][:, :], AF.Identity,
                    bias=bias_col(oc)), deps=[tm])
                bk.took(b, ev)

    for H in range(2):
        dH = c.seal()
        own = lambda t0, t1: [[xn[:, ch, t0 + g * 512: t0 + (g + 1) * 512] for ch in range(DC)]
                              for g in range((t1 - t0) // 512)]
        qsrc = own(H * 1024, (H + 1) * 1024)
        proj_fm(q_sb, 0, 8, qsrc, lambda oc: bqk[:, oc:oc + 1], dH)
        if H == 0:
            ksrc = [[hx[:, ch, 0:512] for ch in range(DC)]] + own(0, 1024)
        else:
            ksrc = own(512, 2048)
        proj_fm(k_sb, 8, 8, ksrc, lambda oc: bqk[:, 8 + oc:9 + oc], c.seal())
        dproj = c.seal()
        emit_qk_ln(p, q_sb, 8, 1024, qgam, qbet, sqscr)
        emit_head_maxnorm(p, q_sb, 8, 1024, c.hnq, sqscr)
        emit_qk_ln(p, k_sb, 8, 1536, kgam, kbet, sqscr)
        emit_head_maxnorm(p, k_sb, 8, 1536, c.hnk, sqscr)
        dn = c.seal()
        c.emit("vector", lambda v: v.tensor_mul(c.hnc[:, :], c.hnq[:, :], c.hnk[:, :]), deps=dn)
        c.emit("scalar", lambda s: s.activation(c.hnc[:, :], c.hnc[:, :], AF.Sqrt), deps=c.seal())
        c.emit("vector", lambda v: v.tensor_scalar(c.hnc[:, :], c.hnc[:, :], -0.125, None, ALU.mult), deps=c.seal())
        dn4 = c.seal()
        st1 = c.dma("sync", ncd.ap()[0:8].unsqueeze(0), c.hnc[0:1, :], deps=dn4)
        st2 = c.dma("sync", ncd.ap()[8:16].unsqueeze(0), c.hnc[1:2, :], deps=dn4)
        ld1 = c.dma("sync", c.r2[0:1, 0:16], ncd.ap()[0:16].unsqueeze(0), deps=[st1, st2])

        def mmc(t):
            last = None
            for h in range(16):
                col = (h % 2) * 8 + h // 2
                last = t.matmul(c.ps[6][:, h:h + 1], c.ones_row_f[0:1, :], c.r2[0:1, col:col + 1],
                                start=True, stop=True)
            return last
        tc_ = c.emit("tensor", mmc, deps=[ld1])
        evc = c.emit("vector", lambda v: v.tensor_copy(cbias[:, :], c.ps[6][:, 0:16]), deps=[tc_])
        mset = c.emit("vector", lambda v: v.memset(v_sb[:, :, :, 64:65], 1.0), deps=dH)
        vbk = Banks(c, range(0, 4))
        vread = None
        for vh in range(2):
            wdep = ([vread] if vread else []) + dproj
            lw = c.dma("gpsimd", wv_sb[:, :, :],
                       Wqkv_ap[:, 2048 + vh * 512:2048 + (vh + 1) * 512].rearrange("(c p) h -> p c h", p=128), deps=wdep)
            lb = c.dma("gpsimd", brow[0:1, :], bv_ap[0:1, vh * 512:(vh + 1) * 512], deps=wdep)
            for ktl in range(12):
                gi, off = ktl // 4, (ktl % 4) * 128
                b, war = vbk.take()

                def mmv(t, gi=gi, off=off, b=b, ks=ksrc):
                    for ch in range(DC):
                        t.matmul(c.ps[b][:, :], ks[gi][ch][:, off:off + 128], wv_sb[:, ch, :],
                                 start=(ch == 0), stop=False)
                    return t.matmul(c.ps[b][:, :], c.ones_row_bf[0:1, :], brow[0:1, :], start=False, stop=True)
                tm = c.emit("tensor", mmv, deps=[lw, lb] + war + dproj)
                vread = tm
                ev = c.emit("vector", lambda v, ktl=ktl, vh=vh, b=b: v.tensor_copy(
                    v_sb[:, ktl, vh * 8:(vh + 1) * 8, 0:64],
                    c.ps[b][:, :].rearrange("p (h e) -> p h e", h=8)), deps=[tm, mset])
                vbk.took(b, ev)
        dv = c.seal()
        if getattr(p, 'dbg_l0', None) is not None and H == 0:
            qd, kd, vd = p.dbg_l0
            pst = None

            def dump(src_ap, dst_ap, width):
                nonlocal pst
                cv = c.emit("vector", lambda v: v.tensor_copy(c.tmpf[:, 0:width], src_ap), deps=dv + ([pst] if pst else []))
                pst = c.dma("sync", dst_ap, c.tmpf[:, 0:width], deps=[cv])
            for chd in range(8):
                dump(q_sb[:, chd, 0:512], qd[chd * 128:(chd + 1) * 128, 0:512], 512)
                dump(q_sb[:, chd, 512:1024], qd[chd * 128:(chd + 1) * 128, 512:1024], 512)
                for g in range(3):
                    dump(k_sb[:, chd, g * 512:(g + 1) * 512], kd[chd * 128:(chd + 1) * 128, g * 512:(g + 1) * 512], 512)
            for ktl in range(12):
                dump(v_sb[:, ktl, 0:2, :].rearrange("p a b -> p (a b)"), vd[ktl * 128:(ktl + 1) * 128, 0:130], 130)
            dv = c.seal()
        obk = Banks(c, [4, 5, 6, 7])
        sbk = Banks(c, [0, 1, 2, 3])
        ptk = Banks(c, range(8))
        last_st = None
        for qb in range(2):
            for hg in range(8):
                bo0, owar0 = obk.take()
                bo1, owar1 = obk.take()
                last_av = {}
                for r in range(8):
                    ktl = qb * 4 + r
                    jmin, jmax = max(0, r - 4), min(3, r)
                    qoff, width = jmin * 128, (jmax - jmin + 1) * 128
                    bs0, swar0 = sbk.take()
                    bs1, swar1 = sbk.take()

                    def mms(t, hg=hg, ktl=ktl, qoff=qoff, width=width, qb=qb, bs0=bs0, bs1=bs1):
                        qs = slice(qb * 512 + qoff, qb * 512 + qoff + width)
                        t.matmul(c.ps[bs0][:, 0:width], k_sb[0:64, hg, ktl * 128:(ktl + 1) * 128],
                                 q_sb[0:64, hg, qs], start=True, stop=True, tile_position=(0, 0))
                        return t.matmul(c.ps[bs1][:, 0:width], k_sb[64:128, hg, ktl * 128:(ktl + 1) * 128],
                                        q_sb[64:128, hg, qs], start=True, stop=True, tile_position=(64, 0))
                    tm = c.emit("tensor", mms, deps=[evc] + swar0 + swar1 + dv)
                    for hh, bs in ((0, bs0), (1, bs1)):
                        head = hg * 2 + hh
                        pb, pwar = ptk.take()
                        ex = c.emit("scalar", lambda s, bs=bs, head=head, pb=pb, width=width: s.activation(
                            ptbuf[:, pb, 0:width], c.ps[bs][:, 0:width], AF.Exp,
                            bias=cbias[:, head:head + 1], scale=0.125), deps=[tm] + pwar)
                        sbk.took(bs, ex)
                        mops = [ex]
                        if r <= 3:
                            jj = (r - jmin) * 128
                            mops.append(c.emit("vector", lambda v, pb=pb, jj=jj: v.tensor_mul(
                                ptbuf[:, pb, jj:jj + 128], ptbuf[:, pb, jj:jj + 128], m0_sb[:, :]), deps=[ex]))
                        else:
                            jj = (r - 4 - jmin) * 128
                            mops.append(c.emit("vector", lambda v, pb=pb, jj=jj: v.tensor_mul(
                                ptbuf[:, pb, jj:jj + 128], ptbuf[:, pb, jj:jj + 128], m4_sb[:, :]), deps=[ex]))
                        if H == 0 and qb == 0 and r <= 3:
                            mops.append(c.emit("vector", lambda v, pb=pb, width=width: v.tensor_scalar(
                                ptbuf[:, pb, 0:width], ptbuf[:, pb, 0:width], hfl[:, 0:1], None, ALU.mult), deps=[ex]))
                        bo = bo0 if hh == 0 else bo1
                        owar = (owar0 if hh == 0 else owar1) if r == 0 else []

                        def mav(t, ktl=ktl, head=head, pb=pb, bo=bo, qoff=qoff, width=width, r=r):
                            return t.matmul(c.ps[bo][0:65, qoff:qoff + width], v_sb[:, ktl, head, :],
                                            ptbuf[:, pb, 0:width], start=(r == 0), stop=(r == 7))
                        av = c.emit("tensor", mav, deps=mops + owar)
                        ptk.took(pb, av)
                        last_av[hh] = av
                for hh, bo in ((0, bo0), (1, bo1)):
                    head = hg * 2 + hh
                    rc = c.emit("vector", lambda v, bo=bo: v.reciprocal(
                        c.r2[0:1, 512:1024], c.ps[bo][64:65, 0:512]), deps=[last_av[hh]])
                    bb, bwar = sbk.take()
                    tb = c.emit("tensor", lambda t, bb=bb: t.matmul(
                        c.ps[bb][0:64, :], c.ones_row_f[0:1, 0:64], c.r2[0:1, 512:1024],
                        start=True, stop=True), deps=[rc] + bwar)
                    cbc = c.emit("vector", lambda v, bb=bb: v.tensor_copy(c.tmpf[0:64, :], c.ps[bb][0:64, :]),
                                 deps=[tb] + ([last_st] if last_st else []))
                    sbk.took(bb, cbc)
                    ot = c.emit("vector", lambda v, bo=bo: v.tensor_mul(
                        c.tmpf2.ap().bitcast(BF16)[0:64, 0:512], c.ps[bo][0:64, 0:512],
                        c.tmpf[0:64, :]), deps=[cbc])
                    st = c.dma("sync", o_dram.ap()[head * 64:(head + 1) * 64,
                                                   H * 1024 + qb * 512:H * 1024 + (qb + 1) * 512],
                               c.tmpf2.ap().bitcast(BF16)[0:64, 0:512], deps=[ot])
                    obk.took(bo, ot)
                    last_st = st
    dall = c.seal()
    wo_sb = FB[:, 0:2048].rearrange("p (b c h) -> p b c h", b=2, c=DC)
    orhs = FB[:, 2048:6144].rearrange("p (c t) -> p c t", c=8)
    sbuf_s = FB[:, 6144:6656]
    bk = Banks(c, range(0, 4))
    wld, wrd = {}, {}
    prev_st = None
    for tcg in range(4):
        tsl = slice(tcg * 512, (tcg + 1) * 512)
        lo = [c.dma("sync", orhs[:, ch, :], o_dram.ap()[ch * 128:(ch + 1) * 128, tsl], deps=dall)
              for ch in range(DC)]
        for dmc in range(8):
            buf = dmc % 2
            war = [wrd[buf]] if buf in wrd else dall
            wld[buf] = c.dma("gpsimd", wo_sb[:, buf],
                             Wout_ap[:, dmc * 128:(dmc + 1) * 128].rearrange("(c p) h -> p c h", p=128), deps=war)
            b, warb = bk.take()

            def mmo(t, buf=buf, b=b):
                last = None
                for ch in range(DC):
                    last = t.matmul(c.ps[b][:, :], wo_sb[:, buf, ch], orhs[:, ch, :],
                                    start=(ch == 0), stop=(ch == DC - 1))
                return last
            tm = c.emit("tensor", mmo, deps=[wld[buf]] + warb + lo)
            wrd[buf] = tm
            ev = c.emit("scalar", lambda s, dmc=dmc, b=b: s.activation(
                sbuf_s[:, :], c.ps[b][:, :], AF.Identity, bias=bout[:, dmc:dmc + 1]),
                deps=[tm] + ([prev_st] if prev_st else []))
            st = c.dma("sync", s_dram.ap()[dmc * 128:(dmc + 1) * 128, tsl], sbuf_s[:, :], deps=[ev])
            prev_st = st
            bk.took(b, st)
        dall = c.seal()
    return c.seal()


def emit_layer0(p, x_in, x_out):
    c = p.ctx
    Wqkv = p.inp("L0Wqkv", [1024, 3072], F32, lambda ci, cc: ci["L0_attn_W_qkv"])
    bqk = p.inp("L0bqk", [128, 16], F32,
                lambda ci, cc: np.ascontiguousarray(ci["L0_attn_b_qkv"][0:2048].reshape(16, 128).T))
    bv = p.inp("L0bv", [1, 1024], F32, lambda ci, cc: ci["L0_attn_b_qkv"][2048:3072][None, :])
    qkgb = p.inp("L0qkgb", [128, 4], F32,
                 lambda ci, cc: np.stack([np.tile(ci["L0_attn_q_gamma"], 2), np.tile(ci["L0_attn_q_beta"], 2),
                                          np.tile(ci["L0_attn_k_gamma"], 2), np.tile(ci["L0_attn_k_beta"], 2)], 1).astype(np.float32))
    Wout = p.inp("L0Wout", [1024, 1024], F32, lambda ci, cc: ci["L0_attn_W_out"])
    bout = p.inp("L0bout", [128, 8], F32,
                 lambda ci, cc: np.ascontiguousarray(ci["L0_attn_b_out"].reshape(8, 128).T))
    hflag = p.inp("L0hflag", [128, 1], F32, lambda ci, cc: np.full((128, 1), float(cc % 2), np.float32))
    m0 = p.inp("L0m0", [128, 128], BF16, lambda ci, cc: np.tril(np.ones((128, 128), BF), -1))
    m4 = p.inp("L0m4", [128, 128], BF16, lambda ci, cc: np.triu(np.ones((128, 128), BF), 0))
    w1 = p.inp("L0w1", [10, 1024, 512], F32,
               lambda ci, cc: np.concatenate([ci["L0_ffn_routed_w1"], ci["L0_ffn_shared_w1"]], 0))
    w3 = p.inp("L0w3", [10, 1024, 512], F32,
               lambda ci, cc: np.concatenate([ci["L0_ffn_routed_w3"], ci["L0_ffn_shared_w3"]], 0))
    w2 = p.inp("L0w2", [10, 512, 1024], F32,
               lambda ci, cc: np.concatenate([ci["L0_ffn_routed_w2"], ci["L0_ffn_shared_w2"]], 0))
    rw = p.inp("L0rw", [1024, 8], F32, lambda ci, cc: ci["L0_ffn_router_w"])
    rb = p.inp("L0rb", [1, 8], F32, lambda ci, cc: ci["L0_ffn_router_b"][None, :])
    halo_in = c.dram("l0_halo_in", [1024, 512], BF16)
    halo_ag = c.dram("l0_halo_ag", [2048, 512], BF16)
    s_dram = c.dram("l0_s", [1024, 2048], BF16)
    o_dram = c.dram("l0_o", [1024, 2048], BF16)
    qkgb_sb = c.sbuf("l0_qkgb", [128, 4], F32)
    d0 = c.seal()
    c.dma("sync", qkgb_sb[:, :], qkgb[:, :], deps=d0)
    emit_ln(p, x_in)
    emit_row_to_col(p, c.r1, c.rstd_col, 6, TT, c.seal())
    xf = c.PF.ap()[:, :].rearrange("p (c t) -> p c t", c=DC)
    scr = c.PB.ap()[:, 0:4096].bitcast(F32)
    emit_router_f32(p, xf, rw, rb, 8, c.logits, c.rstd_col, scr)
    emit_topk2_gates(p, c.logits, c.gates, scr[:, 512:1024])
    l0_attn(p, c.xn, Wqkv, bqk, bv, qkgb_sb[:, 0:1], qkgb_sb[:, 1:2], qkgb_sb[:, 2:3], qkgb_sb[:, 3:4],
            Wout, bout, hflag, m0, m4, halo_in, halo_ag, s_dram, o_dram)
    y_acc = c.PF.ap()[:, :].rearrange("p (t d) -> p t d", t=TT)
    moe_ffn(p, c.xn, w1, w3, w2, c.gates, y_acc, NE=10, NROUTED=8, H=512)
    g_tm = c.PA.ap()[:, 0:TT * 1024].rearrange("p (t d) -> p t d", t=TT)
    emit_silu_tm(p, y_acc, g_tm)
    emit_combine(p, x_in, x_out, s_dram, g_tm)
    return c.seal()


def emit_topk2_gates_l2(p, logits_sb, gates_sb, scr, bias_sb):
    c = p.ctx
    ne = 8
    n = TT * ne
    L = logits_sb[:, 0:n].rearrange("p (t e) -> p t e", t=TT)
    S1 = scr[:, 0:n].rearrange("p (t e) -> p t e", t=TT)
    S2 = scr[:, n:2 * n].rearrange("p (t e) -> p t e", t=TT)
    m1 = scr[:, 2 * n:2 * n + TT]
    m2 = scr[:, 2 * n + TT:2 * n + 2 * TT]
    G = gates_sb[:, 0:n].rearrange("p (t e) -> p t e", t=TT)
    d = c.seal()

    def b(ap):
        return ap.unsqueeze(2).broadcast_to([128, TT, ne])
    bb = bias_sb.unsqueeze(1).broadcast_to([128, TT, ne])
    c.emit("vector", lambda v: v.reduce_max(m1, L, axis=AX.X), deps=d)
    c.emit("vector", lambda v: v.tensor_tensor(S1, L, b(m1), ALU.subtract))
    dv = c.seal()
    c.emit("scalar", lambda s: s.activation(S1, S1, AF.Exp), deps=dv)
    dv = c.seal()
    c.emit("vector", lambda v: v.reduce_sum(m2, S1, axis=AX.X), deps=dv)
    c.emit("vector", lambda v: v.reciprocal(m1, m2))
    c.emit("vector", lambda v: v.tensor_tensor(S1, S1, b(m1), ALU.mult))
    c.emit("vector", lambda v: v.tensor_tensor(S2, S1, bb, ALU.add))
    c.emit("vector", lambda v: v.reduce_max(m1, S2, axis=AX.X))
    c.emit("vector", lambda v: v.tensor_tensor(G, S2, b(m1), ALU.is_ge))
    c.emit("vector", lambda v: v.tensor_scalar(G, G, -1e30, None, ALU.mult))
    c.emit("vector", lambda v: v.tensor_tensor(G, G, S2, ALU.add))
    c.emit("vector", lambda v: v.reduce_max(m2, G, axis=AX.X))
    c.emit("vector", lambda v: v.tensor_tensor(G, S2, b(m2), ALU.is_ge))
    c.emit("vector", lambda v: v.tensor_tensor(G, G, S1, ALU.mult))
    return c.seal()


def l2_attn(p, xn, WqT, bq, WkT, bk, WvT, bvrow, WoT, bo, hflag_ap,
            m4_ap, kv_in, kv_ag, s_dram):
    c = p.ctx
    FB = c.PF.ap().bitcast(BF16)
    q_sb = FB[:, 0:2048]
    ko_sb = FB[:, 2048:4096]
    kp_sb = FB[:, 4096:6144]
    v_sb = FB[:, 6144:10304].rearrange("p (k e) -> p k e", k=32)
    wtile = FB[:, 10304:12352].rearrange("p (b c h) -> p b c h", b=2, c=DC)
    o_sb = FB[:, 12352:14400]
    pt = FB[:, 14400:18496].rearrange("p (b t) -> p b t", b=8)
    m4_sb = FB[:, 18496:18624]
    brow = FB[:, 18624:18752]
    sbuf_s = FB[:, 18752:19264]
    bqk = c.smallf[:, 0:3]
    bo_sb = c.smallf[:, 8:16]
    cbias = c.smallf[:, 16:17]
    d0 = c.seal()
    c.dma("sync", m4_sb[:, :], m4_ap[:, :], deps=d0)
    c.dma("sync", bqk[:, 0:1], bq[:, :], deps=d0)
    c.dma("sync", bqk[:, 1:2], bk[:, :], deps=d0)
    c.dma("sync", bqk[:, 2:3], hflag_ap[:, :], deps=d0)
    c.dma("sync", bo_sb[:, :], bo[:, :], deps=d0)
    c.dma("gpsimd", brow[0:1, :], bvrow[0:1, :], deps=d0)
    wbk = Banks(c, range(0, 4))

    def proj_one(dst_fm, W_ap, bias_col, dep):
        wld, wrd = {}, {}
        toks = []
        buf = proj_one.bufc % 2
        proj_one.bufc += 1
        war = dep + ([proj_one.wrd[buf]] if buf in proj_one.wrd else [])
        ld = c.dma("gpsimd", wtile[:, buf], W_ap[:, :].rearrange("(c p) h -> p c h", p=128), deps=war)
        for tc in range(4):
            b, warb = wbk.take()

            def mm(t, tc=tc, b=b, buf=buf):
                last = None
                for ch in range(DC):
                    last = t.matmul(c.ps[b][:, :], wtile[:, buf, ch], xn[:, ch, tc * 512:(tc + 1) * 512],
                                    start=(ch == 0), stop=(ch == DC - 1))
                return last
            tm = c.emit("tensor", mm, deps=[ld] + warb + dep)
            proj_one.wrd[buf] = tm
            ev = c.emit("scalar", lambda s, tc=tc, b=b: s.activation(
                dst_fm[:, tc * 512:(tc + 1) * 512], c.ps[b][:, :], AF.Identity, bias=bias_col), deps=[tm])
            wbk.took(b, ev)
            toks.append(ev)
        return toks
    proj_one.bufc = 0
    proj_one.wrd = {}
    dq = c.seal()
    proj_one(q_sb, WqT, bqk[:, 0:1], dq)
    proj_one(ko_sb, WkT, bqk[:, 1:2], dq)
    wv = FB[:, 19264:20288].rearrange("p (c h) -> p c h", c=DC)
    vld = c.dma("gpsimd", wv, WvT[:, :].rearrange("(c p) h -> p c h", p=128), deps=c.seal())
    mset = c.emit("vector", lambda v: v.memset(v_sb[:, :, 128:130], 0.0), deps=d0)
    mset = c.emit("vector", lambda v: v.memset(v_sb[:, :, 128:129], 1.0), deps=[mset])
    vbk = Banks(c, range(4, 8))
    for tt in range(TT):
        b, war = vbk.take()

        def mmv(t, tt=tt, b=b):
            for ch in range(DC):
                t.matmul(c.ps[b][:, 0:128], xn[:, ch, tt * 128:(tt + 1) * 128], wv[:, ch, :],
                         start=(ch == 0), stop=False)
            return t.matmul(c.ps[b][:, 0:128], c.ones_row_bf[0:1, :], brow[0:1, :], start=False, stop=True)
        tm = c.emit("tensor", mmv, deps=[vld] + war)
        ev = c.emit("vector", lambda v, tt=tt, b=b: v.tensor_copy(v_sb[:, 16 + tt, 0:128], c.ps[b][:, 0:128]), deps=[tm, mset])
        vbk.took(b, ev)
    dkv = c.seal()
    for tc in range(4):
        c.dma("sync", kv_in.ap()[tc * 65536:(tc + 1) * 65536].rearrange("(p t) -> p t", p=128),
              ko_sb[:, tc * 512:(tc + 1) * 512], deps=dkv)
    for tt in range(TT):
        c.dma("sync", kv_in.ap()[262144 + tt * 16640:262144 + (tt + 1) * 16640].rearrange("(p t) -> p t", p=128),
              v_sb[:, 16 + tt, 0:130], deps=dkv)
    dsend = c.seal()
    cc = c.emit("gpsimd", lambda g: g.collective_compute(
        "AllGather", ALU.bypass, replica_groups=[[0, 1], [2, 3], [4, 5], [6, 7]],
        ins=[kv_in.ap()[:]], outs=[kv_ag.ap()[:]]), deps=dsend, n_inc=1)
    for tc in range(4):
        c.dma("sync", kp_sb[:, tc * 512:(tc + 1) * 512],
              kv_ag.ap()[tc * 65536:(tc + 1) * 65536].rearrange("(p t) -> p t", p=128), deps=[cc])
    for tt in range(TT):
        c.dma("sync", v_sb[:, tt, 0:130],
              kv_ag.ap()[262144 + tt * 16640:262144 + (tt + 1) * 16640].rearrange("(p t) -> p t", p=128), deps=[cc])
    dpeer = c.seal()
    sq = FB[:, 20288:20800]
    hn = c.qks0
    bk2 = Banks(c, range(0, 4))
    for i, srcb in enumerate([q_sb, ko_sb, kp_sb]):
        dst = 0 if i == 0 else 2
        for tc in range(4):
            first = (tc == 0 and i in (0, 1))
            sqt = c.emit("scalar", lambda s, srcb=srcb, tc=tc: s.activation(
                sq[:, 0:512], srcb[:, tc * 512:(tc + 1) * 512], AF.Square), deps=dpeer)
            b, war = bk2.take()
            tm = c.emit("tensor", lambda t, b=b: t.matmul(c.ps[b][0:1, :], c.ones_col_bf[:, :], sq[:, 0:512], start=True, stop=True), deps=[sqt] + war)
            if first:
                ev = c.emit("vector", lambda v, b=b, dst=dst: v.reduce_max(hn[0:1, dst:dst + 1], c.ps[b][0:1, :], axis=AX.X), deps=[tm])
            else:
                ev = c.emit("vector", lambda v, b=b: v.reduce_max(hn[0:1, 1:2], c.ps[b][0:1, :], axis=AX.X), deps=[tm])
                ev = c.emit("vector", lambda v, dst=dst: v.tensor_max(hn[0:1, dst:dst + 1], hn[0:1, dst:dst + 1], hn[0:1, 1:2]))
            bk2.took(b, ev)
    dc_ = c.seal()
    SC = 1.0 / np.sqrt(128.0)
    c.emit("vector", lambda v: v.tensor_mul(hn[0:1, 1:2], hn[0:1, 0:1], hn[0:1, 2:3]), deps=dc_)
    c.emit("scalar", lambda s: s.activation(hn[0:1, 1:2], hn[0:1, 1:2], AF.Sqrt), deps=c.seal())
    c.emit("vector", lambda v: v.tensor_scalar(hn[0:1, 1:2], hn[0:1, 1:2], -SC, None, ALU.mult), deps=c.seal())
    dcb = c.seal()
    tb = c.emit("tensor", lambda t: t.matmul(c.ps[0][:, 0:1], c.ones_row_f[0:1, :], hn[0:1, 1:2], start=True, stop=True), deps=dcb)
    evc = c.emit("vector", lambda v: v.tensor_copy(cbias[:, :], c.ps[0][:, 0:1]), deps=[tb])
    dall = c.seal()
    sbk = Banks(c, range(0, 4))
    obk = Banks(c, [4, 5, 6, 7])
    ptk = Banks(c, range(8))
    kall = [kp_sb, ko_sb]
    for qb in range(4):
        bo, owar = obk.take()
        bn, nwar = obk.take()
        nkt = 16 + qb * 4 + 4
        last_av = None
        for kt in range(nkt):
            blk, ktl = (0, kt) if kt < 16 else (1, kt - 16)
            bs, swar = sbk.take()

            def mms(t, blk=blk, ktl=ktl, qb=qb, bs=bs):
                return t.matmul(c.ps[bs][:, :], kall[blk][:, ktl * 128:(ktl + 1) * 128],
                                q_sb[:, qb * 512:(qb + 1) * 512], start=True, stop=True)
            tm = c.emit("tensor", mms, deps=[evc] + swar + dall)
            pb, pwar = ptk.take()
            ex = c.emit("scalar", lambda s, bs=bs, pb=pb: s.activation(
                pt[:, pb, :], c.ps[bs][:, :], AF.Exp, bias=cbias[:, 0:1], scale=SC), deps=[tm] + pwar)
            sbk.took(bs, ex)
            mops = [ex]
            if kt < 16:
                mops.append(c.emit("vector", lambda v, pb=pb: v.tensor_scalar(
                    pt[:, pb, :], pt[:, pb, :], bqk[:, 2:3], None, ALU.mult), deps=[ex]))
            if kt - 16 >= qb * 4:
                jj = (kt - 16 - qb * 4) * 128
                mops.append(c.emit("vector", lambda v, pb=pb, jj=jj: v.tensor_mul(
                    pt[:, pb, jj:jj + 128], pt[:, pb, jj:jj + 128], m4_sb[:, :]), deps=[ex]))
                if jj > 0:
                    mops.append(c.emit("vector", lambda v, pb=pb, jj=jj: v.memset(
                        pt[:, pb, 0:jj], 0.0), deps=[ex]))

            def mav(t, kt=kt, pb=pb, bo=bo, bn=bn, first=(kt == 0), last=(kt == nkt - 1)):
                t.matmul(c.ps[bo][:, :], v_sb[:, kt, 0:128], pt[:, pb, :], start=first, stop=last)
                return t.matmul(c.ps[bn][0:1, :], v_sb[:, kt, 128:129], pt[:, pb, :], start=first, stop=last)
            av = c.emit("tensor", mav, deps=mops + (owar + nwar if kt == 0 else []))
            ptk.took(pb, av)
            last_av = av
        rc = c.emit("vector", lambda v, bn=bn: v.reciprocal(c.r2[0:1, 0:512], c.ps[bn][0:1, :]), deps=[last_av])
        bb, bwar = sbk.take()
        tb2 = c.emit("tensor", lambda t, bb=bb: t.matmul(c.ps[bb][:, :], c.ones_row_f[0:1, :], c.r2[0:1, 0:512], start=True, stop=True), deps=[rc] + bwar)
        cbc = c.emit("vector", lambda v, bb=bb: v.tensor_copy(c.tmpf[:, :], c.ps[bb][:, :]), deps=[tb2])
        sbk.took(bb, cbc)
        ot = c.emit("vector", lambda v, bo=bo, qb=qb: v.tensor_mul(o_sb[:, qb * 512:(qb + 1) * 512], c.ps[bo][:, :], c.tmpf[:, :]), deps=[cbc])
        obk.took(bo, ot)
        obk.took(bn, ot)
    do = c.seal()
    wo = FB[:, 0:256].rearrange("p (b h) -> p b h", b=2)
    prev_st = None
    wrd2 = {}
    for tc in range(4):
        for dmc in range(8):
            buf = dmc % 2
            war = ([wrd2[buf]] if buf in wrd2 else []) + do
            ld = c.dma("gpsimd", wo[:, buf], WoT[:, dmc * 128:(dmc + 1) * 128], deps=war)
            b, warb = sbk.take()
            tm = c.emit("tensor", lambda t, buf=buf, tc=tc, b=b: t.matmul(
                c.ps[b][:, :], wo[:, buf], o_sb[:, tc * 512:(tc + 1) * 512], start=True, stop=True), deps=[ld] + warb + do)
            wrd2[buf] = tm
            ev = c.emit("scalar", lambda s, dmc=dmc, b=b: s.activation(
                sbuf_s[:, :], c.ps[b][:, :], AF.Identity, bias=bo_sb[:, dmc:dmc + 1]),
                deps=[tm] + ([prev_st] if prev_st else []))
            prev_st = c.dma("sync", s_dram.ap()[dmc * 128:(dmc + 1) * 128, tc * 512:(tc + 1) * 512], sbuf_s[:, :], deps=[ev])
            sbk.took(b, prev_st)
    return c.seal()


def emit_layer2(p, x_in, x_out):
    c = p.ctx
    WqT = p.inp("L2WqT", [1024, 128], F32, lambda ci, cc: np.ascontiguousarray(ci["L2_attn_Wq"].T))
    WkT = p.inp("L2WkT", [1024, 128], F32, lambda ci, cc: np.ascontiguousarray(ci["L2_attn_Wk"].T))
    WvT = p.inp("L2WvT", [1024, 128], F32, lambda ci, cc: np.ascontiguousarray(ci["L2_attn_Wv"].T))
    bq = p.inp("L2bq", [128, 1], F32, lambda ci, cc: ci["L2_attn_bq"][:, None])
    bk = p.inp("L2bk", [128, 1], F32, lambda ci, cc: ci["L2_attn_bk"][:, None])
    bvr = p.inp("L2bvr", [1, 128], F32, lambda ci, cc: ci["L2_attn_bv"][None, :])
    WoT = p.inp("L2WoT", [128, 1024], F32, lambda ci, cc: np.ascontiguousarray(ci["L2_attn_Wo"].T))
    bo = p.inp("L2bo", [128, 8], F32, lambda ci, cc: np.ascontiguousarray(ci["L2_attn_bo"].reshape(8, 128).T))
    hflag = p.inp("L2hflag", [128, 1], F32, lambda ci, cc: np.full((128, 1), float(cc % 2), np.float32))
    m4 = p.inp("L2m4", [128, 128], BF16, lambda ci, cc: np.triu(np.ones((128, 128), BF), 0))
    gwT = p.inp("L2gwT", [1024, 8], F32, lambda ci, cc: np.ascontiguousarray(ci["L2_ffn_gate_w"].T))
    gz = p.inp("L2gz", [1, 8], F32, lambda ci, cc: np.zeros((1, 8), np.float32))
    gbias = p.inp("L2gbias", [128, 8], F32, lambda ci, cc: np.tile(ci["L2_ffn_gate_bias"][None, :], (128, 1)))
    w1 = p.inp("L2w1T", [10, 1024, 1024], F32,
               lambda ci, cc: np.concatenate([ci["L2_ffn_w1"].transpose(0, 2, 1),
                                              ci["L2_ffn_sw1"].T.reshape(1024, 2, 1024).transpose(1, 0, 2)], 0))
    w3 = p.inp("L2w3T", [10, 1024, 1024], F32,
               lambda ci, cc: np.concatenate([ci["L2_ffn_w3"].transpose(0, 2, 1),
                                              ci["L2_ffn_sw3"].T.reshape(1024, 2, 1024).transpose(1, 0, 2)], 0))
    w2 = p.inp("L2w2T", [10, 1024, 1024], F32,
               lambda ci, cc: np.concatenate([ci["L2_ffn_w2"].transpose(0, 2, 1),
                                              ci["L2_ffn_sw2"].T.reshape(2, 1024, 1024)], 0))
    kv_in = c.dram("l2_kv_in", [528384], BF16)
    kv_ag = c.dram("l2_kv_ag", [2 * 528384], BF16)
    s_dram = c.dram("l2_s", [1024, 2048], BF16)
    emit_ln(p, x_in)
    emit_row_to_col(p, c.r1, c.rstd_col, 6, TT, c.seal())
    xf = c.PF.ap()[:, :].rearrange("p (c t) -> p c t", c=DC)
    scr = c.PB.ap()[:, 0:4096].bitcast(F32)
    emit_router_f32(p, xf, gwT, gz, 8, c.logits, c.rstd_col, scr)
    gb_sb = c.sbuf("l2_gb", [128, 8], F32)
    c.dma("sync", gb_sb[:, :], gbias[:, :], deps=c.seal())
    emit_topk2_gates_l2(p, c.logits, c.gates, scr[:, 512:1024], gb_sb[:, :])
    l2_attn(p, c.xn, WqT, bq, WkT, bk, WvT, bvr, WoT, bo, hflag, m4, kv_in, kv_ag, s_dram)
    y_acc = c.PF.ap()[:, :].rearrange("p (t d) -> p t d", t=TT)
    moe_ffn(p, c.xn, w1, w3, w2, c.gates, y_acc, NE=10, NROUTED=8, H=1024)
    g_tm = c.PA.ap()[:, 0:TT * 1024].rearrange("p (t d) -> p t d", t=TT)
    emit_silu_tm(p, y_acc, g_tm)
    emit_combine(p, x_in, x_out, s_dram, g_tm)
    return c.seal()


def emit_quant16(p, src, dst, B1, B2, B3, amax_sc, nblk):
    c = p.ctx
    N = nblk * 16
    g = lambda ap: ap[:, 0:N].rearrange("p (b k) -> p b k", b=nblk)
    am = amax_sc[:, 0:nblk]
    sc = amax_sc[:, nblk:2 * nblk]
    rs = amax_sc[:, 2 * nblk:3 * nblk]
    scb = sc.unsqueeze(2).broadcast_to([128, nblk, 16])
    rsb = rs.unsqueeze(2).broadcast_to([128, nblk, 16])
    d = c.seal()
    c.emit("vector", lambda v: v.reduce_max(am, g(src), axis=AX.X, apply_absolute_value=True), deps=d)
    c.emit("vector", lambda v: v.tensor_scalar(sc, am, 1.0 / 6, 1e-12, ALU.mult, ALU.max))
    c.emit("vector", lambda v: v.reciprocal(rs, sc))
    da = c.seal()
    c.emit("scalar", lambda s: s.activation(B1[:, 0:N], src[:, 0:N], AF.Abs), deps=da)
    db = c.seal()
    c.emit("vector", lambda v: v.tensor_tensor(g(B1), g(B1), rsb, ALU.mult), deps=db)
    c.emit("vector", lambda v: v.tensor_scalar(B2[:, 0:N], B1[:, 0:N], 0.25, None, ALU.is_gt))
    for t in (0.75, 1.25, 1.75):
        c.emit("vector", lambda v, t=t: v.scalar_tensor_tensor(B2[:, 0:N], B1[:, 0:N], t, B2[:, 0:N], ALU.is_gt, ALU.add))
    c.emit("vector", lambda v: v.tensor_scalar(B3[:, 0:N], B1[:, 0:N], 2.5, None, ALU.is_gt))
    c.emit("vector", lambda v: v.scalar_tensor_tensor(B3[:, 0:N], B1[:, 0:N], 3.5, B3[:, 0:N], ALU.is_gt, ALU.add))
    c.emit("vector", lambda v: v.scalar_tensor_tensor(B3[:, 0:N], B1[:, 0:N], 5.0, B3[:, 0:N], ALU.is_gt, ALU.add))
    c.emit("vector", lambda v: v.scalar_tensor_tensor(B3[:, 0:N], B1[:, 0:N], 5.0, B3[:, 0:N], ALU.is_gt, ALU.add))
    c.emit("vector", lambda v: v.scalar_tensor_tensor(B2[:, 0:N], B2[:, 0:N], 0.5, B3[:, 0:N], ALU.mult, ALU.add))
    dc2 = c.seal()
    c.emit("scalar", lambda s: s.activation(B1[:, 0:N], src[:, 0:N], AF.Sign), deps=dc2)
    dd = c.seal()
    c.emit("vector", lambda v: v.tensor_tensor(g(B1), g(B1), scb, ALU.mult), deps=dd)
    return c.emit("vector", lambda v: v.tensor_tensor(dst[:, 0:N], B2[:, 0:N], B1[:, 0:N], ALU.mult))


def l1_attn(p, xn, Wp_ap, xn_in, xn_ag, k_dram, v_dram, s_dram):
    c = p.ctx
    FB = c.PF.ap().bitcast(BF16)
    q_sb = FB[:, 0:16384].rearrange("p (c t) -> p c t", c=8)
    k_sb = FB[:, 16384:20480]
    wq_sb = FB[:, 20480:22528].rearrange("p (b c h) -> p b c h", b=2, c=DC)
    xc_sb = FB[:, 22528:26624].rearrange("p (c t) -> p c t", c=8)
    wv_sb = FB[:, 26624:30720].rearrange("p (c h) -> p c h", c=DC)
    kstage = FB[:, 30720:31232]
    sqb = FB[:, 31232:31744]
    v_sb = c.PB.ap()[:, 0:8320].rearrange("p (b k h e) -> p b k h e", b=2, k=32, h=2)
    pt = c.PB.ap()[:, 8320:11392].rearrange("p (b t) -> p b t", b=6)
    vstage = c.PB.ap()[:, 11392:11392 + 4 * 16 * 65].rearrange("p (k h e) -> p k h e", k=4, h=16)
    cbias = c.smallf[:, 32:48]
    ncd = p.ncd
    d0 = c.seal()
    for ch in range(DC):
        c.dma("sync", xn_in.ap()[ch * 262144:(ch + 1) * 262144].rearrange("(p t) -> p t", p=128),
              xn[:, ch, :], deps=d0)
    dxi = c.seal()
    cc = c.emit("gpsimd", lambda g: g.collective_compute(
        "AllGather", ALU.bypass, replica_groups=[[0, 1], [2, 3], [4, 5], [6, 7]],
        ins=[xn_in.ap()[:]], outs=[xn_ag.ap()[:]]), deps=dxi, n_inc=1)
    dcc = [cc]
    bkq = Banks(c, range(0, 4))
    wrd = {}
    for oc in range(8):
        buf = oc % 2
        war = ([wrd[buf]] if buf in wrd else []) + d0
        ld = c.dma("gpsimd", wq_sb[:, buf], Wp_ap[:, oc * 128:(oc + 1) * 128].rearrange("(c p) h -> p c h", p=128), deps=war)
        for tc in range(4):
            b, warb = bkq.take()

            def mm(t, buf=buf, tc=tc, b=b):
                last = None
                for ch in range(DC):
                    last = t.matmul(c.ps[b][:, :], wq_sb[:, buf, ch], xn[:, ch, tc * 512:(tc + 1) * 512],
                                    start=(ch == 0), stop=(ch == DC - 1))
                return last
            tm = c.emit("tensor", mm, deps=[ld] + warb + d0)
            wrd[buf] = tm
            ev = c.emit("vector", lambda v, oc=oc, tc=tc, b=b: v.tensor_copy(
                q_sb[:, oc, tc * 512:(tc + 1) * 512], c.ps[b][:, :]), deps=[tm])
            bkq.took(b, ev)
    dq = c.seal()
    emit_head_maxnorm(p, q_sb, 8, 2048, c.hnq, sqb)
    kbk = Banks(c, range(0, 4))
    vbk = Banks(c, [4, 5, 6, 7])
    wrd2 = {}
    mset = c.emit("vector", lambda v: v.memset(vstage[:, :, :, 64:65], 1.0), deps=d0)
    kn_init = 0
    prev_vst = []
    prev_kst = None
    for hb in range(2):
        for tg in range(4):
            ta = hb * 2048 + tg * 512
            xwar = [("tensor", c.cnt["tensor"])]
            dldx = [c.dma("sync", xc_sb[:, ch, :],
                          xn_ag.ap()[hb * 2097152 + ch * 262144:hb * 2097152 + (ch + 1) * 262144]
                          .rearrange("(p t) -> p t", p=128)[:, tg * 512:(tg + 1) * 512],
                          deps=dcc + xwar) for ch in range(DC)]
            dx = dldx
            for oc in range(8):
                buf = oc % 2
                war = ([wrd2[buf]] if buf in wrd2 else []) + dx
                ld = c.dma("gpsimd", wq_sb[:, buf],
                           Wp_ap[:, 1024 + oc * 128:1024 + (oc + 1) * 128].rearrange("(c p) h -> p c h", p=128), deps=war)
                b, warb = kbk.take()

                def mmk(t, buf=buf, b=b):
                    last = None
                    for ch in range(DC):
                        last = t.matmul(c.ps[b][:, :], wq_sb[:, buf, ch], xc_sb[:, ch, :],
                                        start=(ch == 0), stop=(ch == DC - 1))
                    return last
                tm = c.emit("tensor", mmk, deps=[ld] + warb + dx)
                wrd2[buf] = tm
                ev = c.emit("vector", lambda v, b=b: v.tensor_copy(kstage, c.ps[b][:, :]),
                            deps=[tm] + ([prev_kst] if prev_kst else []))
                prev_kst = c.dma("sync", k_dram.ap()[oc * 524288:(oc + 1) * 524288]
                                 .rearrange("(p t) -> p t", p=128)[:, ta:ta + 512], kstage, deps=[ev])
                sq2 = c.emit("scalar", lambda s: s.activation(sqb, kstage, AF.Square), deps=[ev])
                b2, warb2 = kbk.take()
                tm2 = c.emit("tensor", lambda t, b2=b2: t.matmul(c.ps[b2][0:2, :], c.bones_col_bf[:, :], sqb,
                                                                start=True, stop=True), deps=[sq2] + warb2)
                if kn_init < 8:
                    evn = c.emit("vector", lambda v, b2=b2, oc=oc: v.reduce_max(c.hnk[0:2, oc:oc + 1], c.ps[b2][0:2, :], axis=AX.X), deps=[tm2])
                    kn_init += 1
                else:
                    evn = c.emit("vector", lambda v, b2=b2: v.reduce_max(c.qks2[:, 0:1], c.ps[b2][0:2, :], axis=AX.X), deps=[tm2])
                    evn = c.emit("vector", lambda v, oc=oc: v.tensor_max(c.hnk[0:2, oc:oc + 1], c.hnk[0:2, oc:oc + 1], c.qks2[:, 0:1]))
                kbk.took(b, prev_kst)
                kbk.took(b2, evn)
            vread = None
            for vh in range(2):
                lwv = c.dma("gpsimd", wv_sb,
                            Wp_ap[:, 2048 + vh * 512:2048 + (vh + 1) * 512].rearrange("(c p) h -> p c h", p=128),
                            deps=([vread] if vread else []) + dx)
                for vtt in range(4):
                    bv, warv = vbk.take()

                    def mmv(t, vtt=vtt, bv=bv):
                        last = None
                        for ch in range(DC):
                            last = t.matmul(c.ps[bv][:, :], xc_sb[:, ch, vtt * 128:(vtt + 1) * 128],
                                            wv_sb[:, ch, :], start=(ch == 0), stop=(ch == DC - 1))
                        return last
                    tmv = c.emit("tensor", mmv, deps=[lwv] + warv + dx)
                    vread = tmv
                    e1 = c.emit("vector", lambda v, vtt=vtt, vh=vh, bv=bv: v.tensor_copy(
                        vstage[:, vtt, vh * 8:(vh + 1) * 8, 0:64],
                        c.ps[bv][:, :].rearrange("p (h e) -> p h e", h=8)), deps=[tmv, mset] + prev_vst)
                    vbk.took(bv, e1)
            dvst = c.seal()
            prev_vst = []
            for hg in range(8):
                stv = c.dma("sync", v_dram.ap()[hg * 532480:(hg + 1) * 532480]
                            .rearrange("(p t) -> p t", p=128)[:, (ta // 128) * 130:(ta // 128 + 4) * 130]
                            .rearrange("p (k h e) -> p k h e", k=4, h=2),
                            vstage[:, :, 2 * hg:2 * hg + 2, :], deps=dvst)
                prev_vst.append(stv)
    dkv = c.seal()
    c.emit("vector", lambda v: v.tensor_mul(c.hnc[:, :], c.hnq[:, :], c.hnk[:, :]), deps=dkv)
    c.emit("scalar", lambda s: s.activation(c.hnc[:, :], c.hnc[:, :], AF.Sqrt), deps=c.seal())
    c.emit("vector", lambda v: v.tensor_scalar(c.hnc[:, :], c.hnc[:, :], -0.125, None, ALU.mult), deps=c.seal())
    dn4 = c.seal()
    st1 = c.dma("sync", ncd.ap()[0:8].unsqueeze(0), c.hnc[0:1, :], deps=dn4)
    st2 = c.dma("sync", ncd.ap()[8:16].unsqueeze(0), c.hnc[1:2, :], deps=dn4)
    ld1 = c.dma("sync", c.r2[0:1, 0:16], ncd.ap()[0:16].unsqueeze(0), deps=[st1, st2])

    def mmc(t):
        last = None
        for h in range(16):
            col = (h % 2) * 8 + h // 2
            last = t.matmul(c.ps[6][:, h:h + 1], c.ones_row_f[0:1, :], c.r2[0:1, col:col + 1], start=True, stop=True)
        return last
    tc_ = c.emit("tensor", mmc, deps=[ld1])
    evc = c.emit("vector", lambda v: v.tensor_copy(cbias[:, :], c.ps[6][:, 0:16]), deps=[tc_])
    dall = c.seal()
    sbk = Banks(c, range(0, 4))
    obk = Banks(c, [4, 5, 6, 7])
    ptk = Banks(c, range(6))
    last_st = None
    kread = None
    vread = None
    for hg in range(8):
        vb = hg % 2
        lk = c.dma("sync", k_sb, k_dram.ap()[hg * 524288:(hg + 1) * 524288].rearrange("(p t) -> p t", p=128),
                   deps=dall + ([kread] if kread else []))
        lv = c.dma("sync", v_sb[:, vb].rearrange("p k h e -> p (k h e)"),
                   v_dram.ap()[hg * 532480:(hg + 1) * 532480].rearrange("(p t) -> p t", p=128),
                   deps=dall + ([vread] if vread else []))
        for qb in range(4):
            bo0, owar0 = obk.take()
            bo1, owar1 = obk.take()
            last_av = {}
            for kt in range(32):
                bs0, swar0 = sbk.take()
                bs1, swar1 = sbk.take()

                def mms(t, hg=hg, kt=kt, qb=qb, bs0=bs0, bs1=bs1):
                    qs = slice(qb * 512, (qb + 1) * 512)
                    t.matmul(c.ps[bs0][:, :], k_sb.rearrange("p (k t) -> p k t", k=32)[0:64, kt, :],
                             q_sb[0:64, hg, qs], start=True, stop=True, tile_position=(0, 0))
                    return t.matmul(c.ps[bs1][:, :], k_sb.rearrange("p (k t) -> p k t", k=32)[64:128, kt, :],
                                    q_sb[64:128, hg, qs], start=True, stop=True, tile_position=(64, 0))
                tm = c.emit("tensor", mms, deps=[evc, lk, lv] + swar0 + swar1)
                kread = tm
                for hh, bs in ((0, bs0), (1, bs1)):
                    head = hg * 2 + hh
                    pb, pwar = ptk.take()
                    ex = c.emit("scalar", lambda s, bs=bs, head=head, pb=pb: s.activation(
                        pt[:, pb, :], c.ps[bs][:, :], AF.Exp, bias=cbias[:, head:head + 1], scale=0.125), deps=[tm] + pwar)
                    sbk.took(bs, ex)
                    bo = bo0 if hh == 0 else bo1
                    owar = (owar0 if hh == 0 else owar1) if kt == 0 else []

                    def mav(t, kt=kt, vb=vb, hh=hh, pb=pb, bo=bo, first=(kt == 0), last=(kt == 31)):
                        return t.matmul(c.ps[bo][0:65, :], v_sb[:, vb, kt, hh, :], pt[:, pb, :],
                                        start=first, stop=last)
                    av = c.emit("tensor", mav, deps=[ex] + owar)
                    ptk.took(pb, av)
                    vread = av
                    last_av[hh] = av
            for hh, bo in ((0, bo0), (1, bo1)):
                head = hg * 2 + hh
                rc = c.emit("vector", lambda v, bo=bo: v.reciprocal(c.r2[0:1, 512:1024], c.ps[bo][64:65, 0:512]), deps=[last_av[hh]])
                bb, bwar = sbk.take()
                tb = c.emit("tensor", lambda t, bb=bb: t.matmul(c.ps[bb][0:64, :], c.ones_row_f[0:1, 0:64], c.r2[0:1, 512:1024], start=True, stop=True), deps=[rc] + bwar)
                cbc = c.emit("vector", lambda v, bb=bb: v.tensor_copy(c.tmpf[0:64, :], c.ps[bb][0:64, :]),
                             deps=[tb] + ([last_st] if last_st else []))
                sbk.took(bb, cbc)
                ot = c.emit("vector", lambda v, bo=bo: v.tensor_mul(
                    c.tmpf2.ap().bitcast(BF16)[0:64, 0:512], c.ps[bo][0:64, 0:512], c.tmpf[0:64, :]), deps=[cbc])
                last_st = c.dma("sync", s_dram.ap()[head * 64:(head + 1) * 64, qb * 512:(qb + 1) * 512],
                                c.tmpf2.ap().bitcast(BF16)[0:64, 0:512], deps=[ot])
                obk.took(bo, ot)
    return c.seal()


def l1_wquant(p, wg_q, wu_q, wd_q, cc_bufs):
    c = p.ctx
    (wg_in, wu_in, wd_in) = cc_bufs
    W = c.PB.ap().bitcast(F32)
    bkw = Banks(c, range(0, 4))
    wg_my = p.inp("L1wg_my", [3, 128, 1024], F32,
                  lambda ci, cc: np.pad(ci["L1_ffn_w_gate"], ((0, 256), (0, 0)))[cc * 384:(cc + 1) * 384].reshape(3, 128, 1024))
    wu_my = p.inp("L1wu_my", [3, 128, 1024], F32,
                  lambda ci, cc: np.pad(ci["L1_ffn_w_up"], ((0, 256), (0, 0)))[cc * 384:(cc + 1) * 384].reshape(3, 128, 1024))
    wd_my = p.inp("L1wd_my", [128, 2816], F32,
                  lambda ci, cc: ci["L1_ffn_w_down"][cc * 128:(cc + 1) * 128])
    src = W[:, 0:2816]
    qbf = W[:, 7040:7744].bitcast(BF16)
    prev_q = None
    for wi, (w_ap, N, ntile, blob) in enumerate([(wg_my, 1024, 3, wg_in), (wu_my, 1024, 3, wu_in), (wd_my, 2816, 1, wd_in)]):
        CH = 1024 if N == 1024 else 1408
        for j in range(ntile):
            d0 = c.seal()
            if ntile == 1:
                ld = c.dma("sync", W[:, 0:N], w_ap[:, :], deps=d0)
            else:
                ld = c.dma("sync", W[:, 0:N], w_ap[j], deps=d0)
            for ci0 in range(N // CH):
                nb = CH // 16
                qt = emit_quant16(p, W[:, ci0 * CH:(ci0 + 1) * CH], qbf,
                                  W[:, 2816:2816 + CH], W[:, 2816 + CH:2816 + 2 * CH],
                                  W[:, 2816 + 2 * CH:2816 + 3 * CH], W[:, 7744:7744 + 3 * nb], nb)
                for tr in range(CH // 128):
                    tra = ci0 * (CH // 128) + tr
                    b, war = bkw.take()
                    tt = c.emit("tensor", lambda t, tr=tr, b=b: t.transpose(
                        c.ps[b].ap().bitcast(BF16)[:, 0:128], qbf[:, tr * 128:(tr + 1) * 128], c.ident_sb[:, :]),
                        deps=[qt] + war)
                    ev = c.emit("vector", lambda v, b=b: v.tensor_copy(
                        c.tmpf.ap().bitcast(BF16)[:, 0:128], c.ps[b].ap().bitcast(BF16)[:, 0:128]),
                        deps=[tt] + ([prev_q] if prev_q else []))
                    bkw.took(b, ev)
                    if ntile == 3:
                        dst = blob.ap()[(j * 8 + tra) * 16384:(j * 8 + tra + 1) * 16384]
                    else:
                        dst = blob.ap()[tra * 16384:(tra + 1) * 16384]
                    prev_q = c.dma("sync", dst.rearrange("(p t) -> p t", p=128),
                                   c.tmpf.ap().bitcast(BF16)[:, 0:128], deps=[ev])
    dsend = c.seal()
    ccs = []
    for blob, out in ((wg_in, wg_q), (wu_in, wu_q), (wd_in, wd_q)):
        cc2 = c.emit("gpsimd", lambda g, blob=blob, out=out: g.collective_compute(
            "AllGather", ALU.bypass, replica_groups=[list(range(8))],
            ins=[blob.ap()[:]], outs=[out.ap()[:]]), deps=dsend, n_inc=1)
        ccs.append(cc2)
    return ccs


def l1_xq(p, mu_col, g_tm_area):
    c = p.ctx
    xf = c.PF.ap()[:, :].rearrange("p (c t) -> p c t", c=DC)
    xq_tm = c.PA.ap()[:, 0:TT * 1024].rearrange("p (t d) -> p t d", t=TT)
    W = c.PB.ap().bitcast(F32)
    xtm = W[:, 0:1024]
    B1 = W[:, 1024:2048]
    B2 = W[:, 2048:3072]
    B3 = W[:, 3072:4096]
    ams = W[:, 4096:4288]
    qbf = W[:, 4288:4800].bitcast(BF16)
    bk = Banks(c, range(0, 8))
    for tt in range(TT):
        d0 = c.seal()
        tev = []
        for chd in range(DC):
            b, war = bk.take()
            tr = c.emit("tensor", lambda t, tt=tt, chd=chd, b=b: t.transpose(
                c.ps[b][:, 0:128], xf[:, chd, tt * 128:(tt + 1) * 128], c.ident_f[:, :]), deps=war + d0)
            ev = c.emit("vector", lambda v, chd=chd, b=b: v.tensor_copy(xtm[:, chd * 128:(chd + 1) * 128], c.ps[b][:, 0:128]), deps=[tr])
            bk.took(b, ev)
            tev.append(ev)
        nv = c.emit("vector", lambda v, tt=tt: v.tensor_scalar(
            xtm[:, :], xtm[:, :], mu_col[:, tt:tt + 1], c.rstd_col[:, tt:tt + 1], ALU.subtract, ALU.mult), deps=tev)
        qt = emit_quant16(p, xtm, qbf, B1, B2, B3, ams, 64)
        c.emit("vector", lambda v, tt=tt: v.tensor_copy(xq_tm[:, tt, :], qbf[:, 0:1024]), deps=[qt])
    return c.seal()


def l1_ffn(p, xq_fm, wg_q, wu_q, wd_q, g_tm, wdeps):
    c = p.ctx
    PFf = c.PF.ap().bitcast(F32)
    gate = PFf[:, 0:2816]
    up = PFf[:, 2816:5632]
    B2 = PFf[:, 5632:8448]
    B3 = PFf[:, 8448:11264]
    ams = PFf[:, 11264:11840]
    hq = PFf[:, 11904:13312].bitcast(BF16)
    hqf = PFf[:, 13312:14720].bitcast(BF16).rearrange("p (i t) -> p i t", i=22)
    wblk = c.PB.ap()[:, 0:8192].rearrange("p (b c h) -> p b c h", b=2, c=DC)
    wdb = c.PB.ap()[:, 8192:10240].rearrange("p (b h) -> p b h", b=2)
    bk = Banks(c, range(0, 4))
    tbk = Banks(c, range(4, 8))
    wld = {}
    wrd = {}
    wdld = {}
    wdrd = {}
    OCG = [(i * 512, 512) for i in range(5)] + [(2560, 256)]
    prev_gtm = None
    for tt in range(TT):
        for gi, (w_q, dst) in enumerate(((wg_q, gate), (wu_q, up))):
            for oi, (o0, ow) in enumerate(OCG):
                buf = (tt * 12 + gi * 6 + oi) % 2
                war = ([wrd[buf]] if buf in wrd else []) + wdeps
                jb0 = o0 // 128
                nj = ow // 128
                for jj in range(nj):
                    wld[buf] = c.dma("sync", wblk[:, buf, :, jj * 128:(jj + 1) * 128],
                                     w_q.ap().rearrange("(j c p o) -> p c j o", j=24, c=DC, p=128)[:, :, jb0 + jj, :],
                                     deps=war)
                b, warb = bk.take()

                def mmg(t, tt=tt, buf=buf, ow=ow, b=b):
                    last = None
                    for ch in range(DC):
                        last = t.matmul(c.ps[b][:, 0:ow], xq_fm[:, ch, tt * 128:(tt + 1) * 128],
                                        wblk[:, buf, ch, 0:ow], start=(ch == 0), stop=(ch == DC - 1))
                    return last
                tm = c.emit("tensor", mmg, deps=[wld[buf]] + warb + wdeps)
                wrd[buf] = tm
                ev = c.emit("vector", lambda v, dst=dst, o0=o0, ow=ow, b=b: v.tensor_copy(
                    dst[:, o0:o0 + ow], c.ps[b][:, 0:ow]), deps=[tm])
                bk.took(b, ev)
        dg = c.seal()
        c.emit("scalar", lambda s: s.activation(B2[:, :], gate[:, :], AF.Silu), deps=dg)
        dh = c.seal()
        c.emit("vector", lambda v: v.tensor_mul(gate[:, :], B2[:, :], up[:, :]), deps=dh)
        qt = emit_quant16(p, gate, hq, up, B2, B3, ams, 176)
        tev = []
        for inc in range(22):
            b, war = tbk.take()
            tr = c.emit("tensor", lambda t, inc=inc, b=b: t.transpose(
                c.ps[b].ap().bitcast(BF16)[:, 0:128], hq[:, inc * 128:(inc + 1) * 128], c.ident_sb[:, :]),
                deps=[qt] + war)
            ev = c.emit("vector", lambda v, inc=inc, b=b: v.tensor_copy(hqf[:, inc, :], c.ps[b].ap().bitcast(BF16)[:, 0:128]), deps=[tr])
            tbk.took(b, ev)
            tev.append(ev)
        bdo0, wardo0 = bk.take()
        bdo1, wardo1 = bk.take()
        mmds = []
        for inc in range(22):
            buf = inc % 2
            war = ([wdrd[buf]] if buf in wdrd else []) + tev
            for rr in range(8):
                wdld[buf] = c.dma("sync", wdb[:, buf][:, rr * 128:(rr + 1) * 128],
                                  wd_q.ap().rearrange("(r i p o) -> p i r o", r=8, i=22, p=128)[:, inc, rr, :],
                                  deps=war)

            def mmd(t, inc=inc, buf=buf, bdo0=bdo0, bdo1=bdo1):
                t.matmul(c.ps[bdo0][:, :], hqf[:, inc, :], wdb[:, buf, 0:512],
                         start=(inc == 0), stop=(inc == 21))
                return t.matmul(c.ps[bdo1][:, :], hqf[:, inc, :], wdb[:, buf, 512:1024],
                                start=(inc == 0), stop=(inc == 21))
            dmm = c.emit("tensor", mmd, deps=[wdld[buf], tev[inc]] + (wardo0 + wardo1 if inc == 0 else []))
            wdrd[buf] = dmm
            mmds.append(dmm)
        e1 = c.emit("scalar", lambda s, tt=tt, bdo0=bdo0: s.activation(g_tm[:, tt, 0:512], c.ps[bdo0][:, :], AF.Silu), deps=[mmds[-1]])
        e2 = c.emit("scalar", lambda s, tt=tt, bdo1=bdo1: s.activation(g_tm[:, tt, 512:1024], c.ps[bdo1][:, :], AF.Silu), deps=[mmds[-1]])
        bk.took(bdo0, e1)
        bk.took(bdo1, e2)
    return c.seal()


def emit_layer1(p, x_in, x_out):
    c = p.ctx
    def permW(ci, cc):
        W = ci["L1_attn_W"].reshape(1024, 16, 192)
        return np.ascontiguousarray(np.concatenate(
            [W[:, :, 0:64].reshape(1024, 1024), W[:, :, 64:128].reshape(1024, 1024),
             W[:, :, 128:192].reshape(1024, 1024)], axis=1))
    Wp = p.inp("L1Wp", [1024, 3072], F32, permW)
    xn_in = c.dram("l1_xn_in", [2097152], BF16)
    xn_ag = c.dram("l1_xn_ag", [2 * 2097152], BF16)
    k_dram = c.dram("l1_k", [8 * 524288], BF16)
    v_dram = c.dram("l1_v", [8 * 532480], BF16)
    s_dram = c.dram("l1_s", [1024, 2048], BF16)
    p.l1_s = s_dram
    wg_in = c.dram("l1_wg_in", [3 * 8 * 16384], BF16)
    wu_in = c.dram("l1_wu_in", [3 * 8 * 16384], BF16)
    wd_in = c.dram("l1_wd_in", [22 * 16384], BF16)
    wg_q = c.dram("l1_wg_q", [24 * 8 * 16384], BF16)
    p.l1_wgq = wg_q
    wu_q = c.dram("l1_wu_q", [24 * 8 * 16384], BF16)
    wd_q = c.dram("l1_wd_q", [8 * 22 * 16384], BF16)
    p.l1_wdq = wd_q
    emit_ln(p, x_in)
    emit_row_to_col(p, c.r1, c.rstd_col, 6, TT, c.seal())
    emit_row_to_col(p, c.r0, c.mu_col, 7, TT, c.seal())
    l1_xq(p, c.mu_col, None)
    ccs = l1_wquant(p, wg_q, wu_q, wd_q, (wg_in, wu_in, wd_in))
    l1_attn(p, c.xn, Wp, xn_in, xn_ag, k_dram, v_dram, s_dram)
    d = c.seal()
    xq_tm = c.PA.ap()[:, 0:TT * 1024].rearrange("p (t d) -> p t d", t=TT)
    bk = Banks(c, range(0, 8))
    for tt in range(TT):
        for chd in range(DC):
            b, war = bk.take()
            tr = c.emit("tensor", lambda t, tt=tt, chd=chd, b=b: t.transpose(
                c.ps[b].ap().bitcast(BF16)[:, 0:128], xq_tm[:, tt, chd * 128:(chd + 1) * 128],
                c.ident_sb[:, :]), deps=war + d)
            ev = c.emit("vector", lambda v, tt=tt, chd=chd, b=b: v.tensor_copy(
                c.xn[:, chd, tt * 128:(tt + 1) * 128], c.ps[b].ap().bitcast(BF16)[:, 0:128]), deps=[tr])
            bk.took(b, ev)
    dxq = c.seal()
    g_tm = c.PA.ap()[:, 0:TT * 1024].rearrange("p (t d) -> p t d", t=TT)
    l1_ffn(p, c.xn, wg_q, wu_q, wd_q, g_tm, dxq + ccs)
    emit_combine(p, x_in, x_out, s_dram, g_tm)
    return c.seal()


def emit_topk2_gates_l3(p, logits_sb, gates_sb, scr):
    c = p.ctx
    ne = 8
    n = TT * ne
    L = logits_sb[:, 0:n].rearrange("p (t e) -> p t e", t=TT)
    S1 = scr[:, 0:n].rearrange("p (t e) -> p t e", t=TT)
    S2 = scr[:, n:2 * n].rearrange("p (t e) -> p t e", t=TT)
    m1 = scr[:, 2 * n:2 * n + TT]
    m2 = scr[:, 2 * n + TT:2 * n + 2 * TT]
    t1 = scr[:, 2 * n + 2 * TT:2 * n + 3 * TT]
    G = gates_sb[:, 0:n].rearrange("p (t e) -> p t e", t=TT)
    d = c.seal()

    def b(ap):
        return ap.unsqueeze(2).broadcast_to([128, TT, ne])
    c.emit("vector", lambda v: v.reduce_max(m1, L, axis=AX.X), deps=d)
    c.emit("vector", lambda v: v.tensor_tensor(S1, L, b(m1), ALU.subtract))
    dv = c.seal()
    c.emit("scalar", lambda s: s.activation(S2, S1, AF.Exp), deps=dv)
    dv = c.seal()
    c.emit("vector", lambda v: v.reduce_sum(m2, S2, axis=AX.X), deps=dv)
    c.emit("vector", lambda v: v.reciprocal(m1, m2))
    c.emit("vector", lambda v: v.tensor_tensor(S1, S2, b(m1), ALU.mult))
    c.emit("vector", lambda v: v.reduce_max(m1, S1, axis=AX.X))
    c.emit("vector", lambda v: v.tensor_tensor(S2, S1, b(m1), ALU.is_ge))
    c.emit("vector", lambda v: v.tensor_scalar(S2, S2, -1e30, None, ALU.mult))
    c.emit("vector", lambda v: v.tensor_tensor(S2, S2, S1, ALU.add))
    c.emit("vector", lambda v: v.reduce_max(m2, S2, axis=AX.X))
    c.emit("vector", lambda v: v.tensor_tensor(S2, S1, b(m2), ALU.is_ge))
    dv = c.seal()
    c.emit("scalar", lambda s: s.activation(G, S1, AF.Exp, scale=0.5), deps=dv)
    c.emit("scalar", lambda s: s.activation(t1, m1, AF.Exp, scale=0.5))
    c.emit("scalar", lambda s: s.activation(m1, m2, AF.Exp, scale=0.5))
    dv = c.seal()
    c.emit("vector", lambda v: v.tensor_add(t1, t1, m1), deps=dv)
    c.emit("vector", lambda v: v.reciprocal(m2, t1))
    c.emit("vector", lambda v: v.tensor_tensor(G, G, S2, ALU.mult))
    c.emit("vector", lambda v: v.tensor_tensor(G, G, b(m2), ALU.mult))
    return c.seal()


def l3_router_f32(p, Wr1T_ap, br1_ap, Wr2_ap, br2_ap, logits_sb):
    c = p.ctx
    xf = c.PF.ap()[:, :].rearrange("p (c t) -> p c t", c=DC)
    W = c.PB.ap().bitcast(F32)
    wr1 = W[:, 4096:6144].rearrange("p (c h) -> p c h", c=DC)
    wr2 = c.sbuf("l3_wr2", [128, 2, 8], F32)
    brs = c.sbuf("l3_brs", [128, 4], F32)
    t1f = W[:, 0:4096].rearrange("p (a t) -> p a t", a=2)
    br1b = c.sbuf("l3_br1b", [128, 2], F32)
    d0 = c.seal()
    for ch in range(DC):
        c.dma("sync", wr1[:, ch, :], Wr1T_ap[ch * 128:(ch + 1) * 128, :], deps=d0)
    c.dma("sync", wr2[:, 0, :], Wr2_ap[0:128, :], deps=d0)
    c.dma("sync", wr2[:, 1, :], Wr2_ap[128:256, :], deps=d0)
    c.dma("sync", br1b[:, :], br1_ap[:, :], deps=d0)
    dw = c.seal()
    tcs = c.emit("tensor", lambda t: [t.matmul(c.ps[4][0:1, 0:256], c.ones_col_f[:, :], wr1[:, ch, :],
                                               start=(ch == 0), stop=(ch == DC - 1)) for ch in range(DC)][-1], deps=dw)
    ev0 = c.emit("vector", lambda v: v.tensor_scalar(c.r2[0:1, 1024:1280], c.ps[4][0:1, 0:256], -1.0, None, ALU.mult), deps=[tcs])
    ttr = c.emit("tensor", lambda t: [t.matmul(c.ps[5][:, a:a + 1], c.r2[0:1, 1024 + a * 128:1024 + (a + 1) * 128],
                                               c.ones_row_f[0:1, 0:1], start=True, stop=True) for a in range(2)][-1], deps=[ev0])
    evc = c.emit("vector", lambda v: v.tensor_copy(brs[:, 0:2], c.ps[5][:, 0:2]), deps=[ttr])
    bk = Banks(c, range(0, 4))
    prev = None
    for a in range(2):
        for tc in range(4):
            sl = slice(tc * 512, (tc + 1) * 512)
            b, war = bk.take()

            def mm(t, a=a, sl=sl, b=b):
                for ch in range(DC):
                    t.matmul(c.ps[b][:, :], wr1[:, ch, a * 128:(a + 1) * 128], xf[:, ch, sl],
                             start=(ch == 0), stop=(ch == DC - 1))
                return t.matmul(c.ps[6][:, :], c.ones_row_f[0:1, :], c.r0[0:1, sl], start=True, stop=True)
            tm = c.emit("tensor", mm, deps=dw + war + [evc] + ([prev] if prev else []))
            e0 = c.emit("vector", lambda v: v.tensor_copy(c.tmpf2[:, :], c.ps[6][:, :]), deps=[tm])
            e1 = c.emit("vector", lambda v, a=a, b=b: v.scalar_tensor_tensor(
                c.tmpf[:, :], c.tmpf2[:, :], brs[:, a:a + 1], c.ps[b][:, :], ALU.mult, ALU.add), deps=[e0])
            tb = c.emit("tensor", lambda t: t.matmul(c.ps[7][:, :], c.ones_row_f[0:1, :], c.r1[0:1, sl], start=True, stop=True), deps=[e1])
            e2 = c.emit("vector", lambda v: v.tensor_mul(c.tmpf[:, :], c.tmpf[:, :], c.ps[7][:, :]), deps=[tb])
            e3 = c.emit("vector", lambda v, a=a, sl=sl: v.tensor_scalar(t1f[:, a, sl], c.tmpf[:, :], br1b[:, a:a + 1], 0.0, ALU.add, ALU.max))
            prev = e3
            bk.took(b, e3)
    dt1 = c.seal()
    br2s = c.sbuf("l3_br2s", [1, 8], F32)
    c.dma("sync", br2s[0:1, :], br2_ap[0:1, :], deps=dt1)
    dbr = c.seal()
    tbr = c.emit("tensor", lambda t: t.matmul(c.ps[5][:, 0:8], c.ones_row_f[0:1, :], br2s[0:1, :], start=True, stop=True), deps=dbr)
    evb = c.emit("vector", lambda v: v.tensor_copy(c.smallf[:, 48:56], c.ps[5][:, 0:8]), deps=[tbr])
    bk2 = Banks(c, range(0, 4))
    for tt in range(TT):
        b, war = bk2.take()

        def mm(t, tt=tt, b=b):
            last = None
            for a in range(2):
                last = t.matmul(c.ps[b][:, 0:8], t1f[:, a, tt * 128:(tt + 1) * 128], wr2[:, a, :],
                                start=(a == 0), stop=(a == 1))
            return last
        tm = c.emit("tensor", mm, deps=dt1 + war + [evb])
        ev = c.emit("vector", lambda v, tt=tt, b=b: v.tensor_tensor(
            logits_sb[:, tt * 8:(tt + 1) * 8], c.ps[b][:, 0:8], c.smallf[:, 48:56], ALU.add), deps=[tm])
        bk2.took(b, ev)
    return c.seal()


def l3_attn(p, xn, WqT, WkT, WvT, WgT, Wgk1T, Wgk2T, bgk2r, WoutT, rmsw_ap, hflag_ap,
            m4_ap, st_in, st_ag, v_dram, o_dram, s_dram):
    c = p.ctx
    GSC = 1.0 / 32.0
    QSC = 256.0 ** -0.5
    FB = c.PF.ap().bitcast(BF16)
    q_sb = FB[:, 0:16384].rearrange("p (c t) -> p c t", c=8)
    k_sb = FB[:, 16384:32768].rearrange("p (c t) -> p c t", c=8)
    PBb = c.PB.ap()
    v_h = PBb[:, 0:8192].rearrange("p (i e) -> p i e", i=16)
    Sf = PBb[:, 8192:10240].bitcast(F32).rearrange("p (a e) -> p a e", a=2)
    Sbf = PBb[:, 10240:11264].rearrange("p (a e) -> p a e", a=2)
    ebuf = PBb[:, 11264:11776].bitcast(F32).rearrange("p (a t) -> p a t", a=2)
    nbuf = PBb[:, 11776:12288].bitcast(F32).rearrange("p (a t) -> p a t", a=2)
    qt_sb = PBb[:, 13312:13568].rearrange("p (a t) -> p a t", a=2)
    kt_sb = PBb[:, 13568:13824].rearrange("p (a t) -> p a t", a=2)
    ktm_sb = PBb[:, 13824:14080].rearrange("p (a t) -> p a t", a=2)
    A_sb = PBb[:, 14080:14208]
    ost = PBb[:, 14208:14720]
    blf = PBb[:, 14720:14976].bitcast(F32)
    elb = PBb[:, 14976:15104].bitcast(F32)
    m4_sb = PBb[:, 15104:15232]
    rmsw = c.smallf[:, 48:52]
    hfl = c.smallf[:, 52:53]
    d0 = c.seal()
    c.dma("sync", m4_sb[:, :], m4_ap[:, :], deps=d0)
    c.dma("sync", rmsw[:, :], rmsw_ap[:, :], deps=d0)
    c.dma("sync", hfl[:, :], hflag_ap[:, :], deps=d0)
    wbuf = c.sbuf("l3_wbuf", [128, 2, DC, 128], BF16)
    wv_sb = c.PA.ap()[:, 0:4096].rearrange("p (c h) -> p c h", c=DC)
    bk = Banks(c, range(0, 4))
    wrd = {}
    bufc = [0]

    def proj_fm(dst, W_ap, oc0, nchunks, scale, dep):
        for oc in range(nchunks):
            buf = bufc[0] % 2
            bufc[0] += 1
            war = ([wrd[buf]] if buf in wrd else []) + dep
            ld = c.dma("gpsimd", wbuf[:, buf],
                       W_ap[:, (oc0 + oc) * 128:(oc0 + oc + 1) * 128].rearrange("(c p) h -> p c h", p=128), deps=war)
            for tc in range(4):
                b, warb = bk.take()

                def mm(t, buf=buf, tc=tc, b=b):
                    last = None
                    for ch in range(DC):
                        last = t.matmul(c.ps[b][:, :], wbuf[:, buf, ch], xn[:, ch, tc * 512:(tc + 1) * 512],
                                        start=(ch == 0), stop=(ch == DC - 1))
                    return last
                tm = c.emit("tensor", mm, deps=[ld] + warb + dep)
                wrd[buf] = tm
                if scale is None:
                    ev = c.emit("vector", lambda v, oc=oc, tc=tc, b=b, dst=dst: v.tensor_copy(
                        dst[:, oc, tc * 512:(tc + 1) * 512], c.ps[b][:, :]), deps=[tm])
                else:
                    ev = c.emit("scalar", lambda s, oc=oc, tc=tc, b=b, dst=dst, scale=scale: s.activation(
                        dst[:, oc, tc * 512:(tc + 1) * 512], c.ps[b][:, :], AF.Identity, scale=scale), deps=[tm])
                bk.took(b, ev)
    proj_fm(q_sb, WqT, 0, 8, QSC, d0)
    proj_fm(k_sb, WkT, 0, 8, None, d0)
    dqk = c.seal()
    vbk = Banks(c, [4, 5])
    wvrd = None
    prev_vst = None
    for h in range(4):
        lw = c.dma("gpsimd", wv_sb, WvT[:, h * 512:(h + 1) * 512].rearrange("(c p) h -> p c h", p=128),
                   deps=([wvrd] if wvrd else []) + dqk)
        for tt in range(TT):
            b, war = vbk.take()

            def mmv(t, tt=tt, b=b):
                last = None
                for ch in range(DC):
                    last = t.matmul(c.ps[b][:, :], xn[:, ch, tt * 128:(tt + 1) * 128], wv_sb[:, ch, :],
                                    start=(ch == 0), stop=(ch == DC - 1))
                return last
            tm = c.emit("tensor", mmv, deps=[lw] + war)
            wvrd = tm
            ev = c.emit("vector", lambda v, b=b: v.tensor_copy(ost[:, :], c.ps[b][:, :]),
                        deps=[tm] + ([prev_vst] if prev_vst else []))
            prev_vst = c.dma("sync", v_dram.ap()[(h * 16 + tt) * 65536:(h * 16 + tt + 1) * 65536]
                             .rearrange("(p e) -> p e", p=128), ost[:, :], deps=[ev])
            vbk.took(b, ev)
    g_tm = c.PA.ap()[:, 0:16384].rearrange("p (t d) -> p t d", t=TT)
    t1g = PBb[0:32, 0:2048]
    wg2 = PBb[0:32, 2048:3072]
    bgk = PBb[0:1, 3072:4096]
    dg0 = c.seal()
    lw1 = c.dma("gpsimd", wbuf[:, 0, :, 0:32], Wgk1T[:, :].rearrange("(c p) h -> p c h", p=128), deps=dg0)
    lw2 = c.dma("gpsimd", wg2[:, :], Wgk2T[:, :], deps=dg0)
    lb2 = c.dma("gpsimd", bgk[0:1, :], bgk2r[0:1, :], deps=dg0)
    gbk = Banks(c, [6, 7])
    for tc in range(4):
        b, war = gbk.take()

        def mmt1(t, tc=tc, b=b):
            last = None
            for ch in range(DC):
                last = t.matmul(c.ps[b][0:32, :], wbuf[:, 0, ch, 0:32], xn[:, ch, tc * 512:(tc + 1) * 512],
                                start=(ch == 0), stop=(ch == DC - 1))
            return last
        tm = c.emit("tensor", mmt1, deps=[lw1] + war + dg0)
        ev = c.emit("vector", lambda v, tc=tc, b=b: v.tensor_copy(t1g[:, tc * 512:(tc + 1) * 512], c.ps[b][0:32, :]), deps=[tm])
        gbk.took(b, ev)
    dt1 = c.seal()
    gbk2 = Banks(c, [4, 5, 6, 7])
    for tt in range(TT):
        b, war = gbk2.take()
        b2, war2 = gbk2.take()

        def mmg(t, tt=tt, b=b, b2=b2):
            last = None
            for half, bb in ((0, b), (1, b2)):
                t.matmul(c.ps[bb][:, :], t1g[:, tt * 128:(tt + 1) * 128], wg2[:, half * 512:(half + 1) * 512],
                         start=True, stop=False)
                last = t.matmul(c.ps[bb][:, :], c.ones_row_bf[0:1, :], bgk[0:1, half * 512:(half + 1) * 512],
                                start=False, stop=True)
            return last
        tm = c.emit("tensor", mmg, deps=[lw2, lb2] + war + war2 + dt1)
        for half, bb in ((0, b), (1, b2)):
            sl = slice(half * 512, (half + 1) * 512)
            e1 = c.emit("scalar", lambda s, bb=bb: s.activation(c.tmpf[:, :], c.ps[bb][:, :], AF.Abs), deps=[tm])
            e2 = c.emit("scalar", lambda s: s.activation(c.tmpf2[:, :], c.tmpf[:, :], AF.Exp, scale=-1.0), deps=[e1])
            e3 = c.emit("vector", lambda v: v.tensor_scalar(c.tmpf[:, :], c.tmpf2[:, :], 1.0, None, ALU.add), deps=[e2])
            e4 = c.emit("scalar", lambda s: s.activation(c.tmpf2[:, :], c.tmpf[:, :], AF.Ln), deps=[e3])
            e5 = c.emit("vector", lambda v, bb=bb: v.tensor_scalar(c.tmpf[:, :], c.ps[bb][:, :], 0.0, None, ALU.min), deps=[tm, e4])
            e6 = c.emit("vector", lambda v, tt=tt, sl=sl: v.tensor_sub(g_tm[:, tt, sl], c.tmpf[:, :], c.tmpf2[:, :]), deps=[e4])
            gbk2.took(bb, e6)
    dgla0 = c.seal()
    chbk = Banks(c, range(0, 8))
    prev_ost = None
    vrd = None
    for h in range(4):
        lv = c.dma("sync", v_h[:, :, :],
                   v_dram.ap()[h * 16 * 65536:(h + 1) * 16 * 65536].rearrange("(i p e) -> p i e", i=16, p=128),
                   deps=dgla0 + ([vrd] if vrd else []))
        ms = c.emit("vector", lambda v: v.memset(PBb[:, 8192:10240], 0.0), deps=dgla0)
        dh0 = c.seal()
        for i in range(TT):
            bb0, w0 = chbk.take()
            bb1, w1 = chbk.take()

            def mmb(t, h=h, i=i, bb0=bb0, bb1=bb1):
                last = None
                for a, bb in ((0, bb0), (1, bb1)):
                    t.matmul(c.ps[bb][:, 0:128], g_tm[:, i, h * 256 + a * 128:h * 256 + (a + 1) * 128],
                             m4_sb[:, :], start=True, stop=True)
                    last = t.matmul(c.ps[bb][:, 128:129], g_tm[:, i, h * 256 + a * 128:h * 256 + (a + 1) * 128],
                                    c.ones_col_bf[:, 0:1], start=True, stop=True)
                return last
            tmb = c.emit("tensor", mmb, deps=dh0 + w0 + w1)
            ee = []
            for a, bb in ((0, bb0), (1, bb1)):
                ee.append(c.emit("scalar", lambda s, a=a, bb=bb: s.activation(ebuf[:, a, :], c.ps[bb][:, 0:128], AF.Exp, scale=GSC), deps=[tmb]))
                ee.append(c.emit("scalar", lambda s, a=a, bb=bb: s.activation(nbuf[:, a, :], c.ps[bb][:, 0:128], AF.Exp, scale=-GSC), deps=[tmb]))
                ee.append(c.emit("vector", lambda v, h=h, i=i, a=a, bb=bb: v.tensor_copy(
                    blf[:, (h * 16 + i) * 2 + a:(h * 16 + i) * 2 + a + 1], c.ps[bb][:, 128:129]), deps=[tmb]))
                ee.append(c.emit("scalar", lambda s, a=a, bb=bb: s.activation(
                    elb[:, a:a + 1], c.ps[bb][:, 128:129], AF.Exp, scale=GSC), deps=[tmb]))
            dsc = c.seal()
            qtoks = []
            for a, bb in ((0, bb0), (1, bb1)):
                qtoks.append(c.emit("vector", lambda v, h=h, i=i, a=a: v.tensor_mul(
                    qt_sb[:, a, :], q_sb[:, h * 2 + a, i * 128:(i + 1) * 128], ebuf[:, a, :]), deps=dsc))
                qtoks.append(c.emit("vector", lambda v, h=h, i=i, a=a: v.tensor_mul(
                    kt_sb[:, a, :], k_sb[:, h * 2 + a, i * 128:(i + 1) * 128], nbuf[:, a, :]), deps=dsc))
            chbk.took(bb0, qtoks[-1])
            chbk.took(bb1, qtoks[-1])
            dqt = qtoks
            bt0, wt0 = chbk.take()
            bt1, wt1 = chbk.take()

            def mmtr(t, bt0=bt0, bt1=bt1):
                t.transpose(c.ps[bt0].ap().bitcast(BF16)[:, 0:128], kt_sb[:, 0, :], c.ident_sb[:, :])
                return t.transpose(c.ps[bt1].ap().bitcast(BF16)[:, 0:128], kt_sb[:, 1, :], c.ident_sb[:, :])
            tmt = c.emit("tensor", mmtr, deps=dqt + wt0 + wt1)
            ev1 = c.emit("vector", lambda v, bt0=bt0: v.tensor_copy(ktm_sb[:, 0, :], c.ps[bt0].ap().bitcast(BF16)[:, 0:128]), deps=[tmt])
            ev2 = c.emit("vector", lambda v, bt1=bt1: v.tensor_copy(ktm_sb[:, 1, :], c.ps[bt1].ap().bitcast(BF16)[:, 0:128]), deps=[tmt])
            chbk.took(bt0, ev1)
            chbk.took(bt1, ev2)
            sb1 = c.emit("vector", lambda v: v.tensor_copy(Sbf[:, 0, :], Sf[:, 0, :]), deps=dh0 if i == 0 else [])
            sb2 = c.emit("vector", lambda v: v.tensor_copy(Sbf[:, 1, :], Sf[:, 1, :]))
            ba, wa = chbk.take()

            def mmA(t, ba=ba):
                t.matmul(c.ps[ba][:, 0:128], kt_sb[:, 0, :], qt_sb[:, 0, :], start=True, stop=False)
                return t.matmul(c.ps[ba][:, 0:128], kt_sb[:, 1, :], qt_sb[:, 1, :], start=False, stop=True)
            tma = c.emit("tensor", mmA, deps=dqt + wa)
            eva = c.emit("vector", lambda v, ba=ba: v.tensor_mul(A_sb[:, :], c.ps[ba][:, 0:128], m4_sb[:, :]), deps=[tma])
            chbk.took(ba, eva)
            bo, wo = chbk.take()
            tmo = c.emit("tensor", lambda t, bo=bo, i=i: _l3_mmo(c, t, bo, v_h, i, A_sb, Sbf, qt_sb),
                         deps=[eva, sb1, sb2, lv] + wo + ([prev_ost] if prev_ost else []))
            evo = c.emit("vector", lambda v, bo=bo: v.tensor_copy(ost[:, :], c.ps[bo][:, :]),
                         deps=[tmo] + ([prev_ost] if prev_ost else []))
            prev_ost = c.dma("sync", o_dram.ap()[(h * 4) * 262144:(h * 4 + 4) * 262144]
                             .rearrange("(d p t) -> p d t", d=4, p=128)[:, :, i * 128:(i + 1) * 128],
                             ost[:, :].rearrange("p (d t) -> p d t", d=4), deps=[evo])
            chbk.took(bo, evo)
            bu0, wu0 = chbk.take()
            bu1, wu1 = chbk.take()

            def mmu(t, i=i, bu0=bu0, bu1=bu1):
                last = None
                for a, bu in ((0, bu0), (1, bu1)):
                    last = t.matmul(c.ps[bu][:, :], ktm_sb[:, a, :], v_h[:, i, :], start=True, stop=True)
                return last
            tmu = c.emit("tensor", mmu, deps=[ev1, ev2, lv] + wu0 + wu1)
            vrd = tmu
            for a, bu in ((0, bu0), (1, bu1)):
                s1 = c.emit("vector", lambda v, a=a, bu=bu: v.tensor_add(Sf[:, a, :], Sf[:, a, :], c.ps[bu][:, :]), deps=[tmu])
                s2 = c.emit("vector", lambda v, a=a: v.tensor_scalar(Sf[:, a, :], Sf[:, a, :], elb[:, a:a + 1], None, ALU.mult), deps=[ee[3 + 4 * a]])
                chbk.took(bu, s2)
        dse = c.seal()
        for a in range(2):
            c.dma("sync", st_in.ap()[(h * 2 + a) * 65536:(h * 2 + a + 1) * 65536].rearrange("(p e) -> p e", p=128),
                  Sf[:, a, :], deps=dse)
    dch = c.seal()
    cc = c.emit("gpsimd", lambda g: g.collective_compute(
        "AllGather", ALU.bypass, replica_groups=[[0, 1], [2, 3], [4, 5], [6, 7]],
        ins=[st_in.ap()[:]], outs=[st_ag.ap()[:]]), deps=dch, n_inc=1)
    ccd = [cc]
    cbk = Banks(c, range(0, 8))

    for h in range(4):
        ls = [c.dma("sync", Sf[:, a, :], st_ag.ap()[(h * 2 + a) * 65536:(h * 2 + a + 1) * 65536]
                    .rearrange("(p e) -> p e", p=128), deps=ccd + [("vector", c.cnt["vector"])]) for a in range(2)]
        hf1 = c.emit("vector", lambda v: v.tensor_scalar(Sf[:, 0, :], Sf[:, 0, :], hfl[:, 0:1], None, ALU.mult), deps=ls)
        hf2 = c.emit("vector", lambda v: v.tensor_scalar(Sf[:, 1, :], Sf[:, 1, :], hfl[:, 0:1], None, ALU.mult))
        sb1 = c.emit("vector", lambda v: v.tensor_copy(Sbf[:, 0, :], Sf[:, 0, :]))
        sb2 = c.emit("vector", lambda v: v.tensor_copy(Sbf[:, 1, :], Sf[:, 1, :]))
        cz = c.emit("vector", lambda v: v.memset(elb[:, 4:8], 0.0))
        dcorr = c.seal()
        cup = cz
        for i in range(TT):
            bb0, w0 = cbk.take()
            bb1, w1 = cbk.take()

            def mmb(t, h=h, i=i, bb0=bb0, bb1=bb1):
                last = None
                for a, bb in ((0, bb0), (1, bb1)):
                    last = t.matmul(c.ps[bb][:, 0:128], g_tm[:, i, h * 256 + a * 128:h * 256 + (a + 1) * 128],
                                    m4_sb[:, :], start=True, stop=True)
                return last
            tmb = c.emit("tensor", mmb, deps=dcorr + w0 + w1 + [cup])
            etoks = []
            for a, bb in ((0, bb0), (1, bb1)):
                etoks.append(c.emit("scalar", lambda s, a=a, bb=bb: s.activation(
                    ebuf[:, a, :], c.ps[bb][:, 0:128], AF.Exp, bias=elb[:, 6 + a:7 + a], scale=GSC), deps=[tmb, cz]))
            chw = []
            for a in range(2):
                chw.append(c.emit("vector", lambda v, h=h, i=i, a=a: v.tensor_mul(
                    qt_sb[:, a, :], q_sb[:, h * 2 + a, i * 128:(i + 1) * 128], ebuf[:, a, :]), deps=etoks))
            cbk.took(bb0, chw[-1])
            cbk.took(bb1, chw[-1])
            bo, wo = cbk.take()

            def mmoc(t, bo=bo):
                last = None
                for d in range(4):
                    for a in range(2):
                        last = t.matmul(c.ps[bo][:, d * 128:(d + 1) * 128], Sbf[:, a, d * 128:(d + 1) * 128],
                                        qt_sb[:, a, :], start=(d == 0 and a == 0), stop=(d == 3 and a == 1))
                return last
            tmo = c.emit("tensor", mmoc, deps=chw + wo + [sb1, sb2, hf1, hf2] + ([prev_ost] if prev_ost else []))
            evo = c.emit("vector", lambda v, bo=bo: v.tensor_copy(ost[:, :], c.ps[bo][:, :]),
                         deps=[tmo] + ([prev_ost] if prev_ost else []))
            prev_ost = c.emit("gpsimd", lambda g, h=h, i=i: g.dma_start(
                out=o_dram.ap()[(h * 4) * 262144:(h * 4 + 4) * 262144]
                .rearrange("(d p t) -> p d t", d=4, p=128)[:, :, i * 128:(i + 1) * 128],
                in_=ost[:, :].rearrange("p (d t) -> p d t", d=4), accum_op=ALU.add), deps=[evo], n_inc=16)
            cbk.took(bo, evo)
            c.emit("vector", lambda v, h=h, i=i: v.tensor_add(
                elb[:, 4:6], elb[:, 4:6], blf[:, (h * 16 + i) * 2:(h * 16 + i) * 2 + 2]))
            cup = c.emit("vector", lambda v: v.tensor_scalar(elb[:, 6:8], elb[:, 4:6], GSC, None, ALU.mult))
    dfix = c.seal()
    ot_buf = PBb[:, 0:1024].rearrange("p (b t) -> p b t", b=2)
    og_all = PBb[:, 1024:9216].rearrange("p (g t) -> p g t", g=16)
    gt_buf = PBb[:, 9216:9728]
    wo_blk = PBb[:, 9728:11776].rearrange("p (b h) -> p b h", b=2)
    sq_b = PBb[:, 11776:12288]
    s_st = PBb[:, 12288:12800]
    wgb = wbuf
    rrow = c.r2
    obk = Banks(c, range(0, 4))
    wbk2 = Banks(c, [4, 5])
    ord_ = None
    word = {}
    gw_rd = {}
    prev_sst = None
    for tc in range(4):
        dtc = c.seal()
        for h in range(4):
            for d in range(4):
                buf = d % 2
                lo = c.dma("sync", ot_buf[:, buf, :],
                           o_dram.ap()[(h * 4 + d) * 262144:(h * 4 + d + 1) * 262144]
                           .rearrange("(p t) -> p t", p=128)[:, tc * 512:(tc + 1) * 512],
                           deps=dtc + ([ord_] if ord_ else []))
                sq2 = c.emit("scalar", lambda s, buf=buf: s.activation(sq_b[:, :], ot_buf[:, buf, :], AF.Square), deps=[lo])
                tms = c.emit("tensor", lambda t, d=d: t.matmul(
                    c.ps[6][0:1, :], c.ones_col_bf[:, :], sq_b[:, :], start=(d == 0), stop=(d == 3)),
                    deps=[sq2] + ([("vector", c.cnt["vector"])] if d == 0 else []))
                ord_ = tms
            e1 = c.emit("vector", lambda v: v.tensor_scalar(c.r2[0:1, 0:512], c.ps[6][0:1, :], 1.0 / 512, EPS, ALU.mult, ALU.add), deps=[ord_])
            e2 = c.emit("scalar", lambda s: s.activation(c.r2[0:1, 512:1024], c.r2[0:1, 0:512], AF.Sqrt), deps=c.seal())
            e3 = c.emit("vector", lambda v: v.reciprocal(c.r2[0:1, 0:512], c.r2[0:1, 512:1024]), deps=c.seal())
            tb = c.emit("tensor", lambda t: t.matmul(c.ps[7][:, :], c.ones_row_f[0:1, :], c.r2[0:1, 0:512], start=True, stop=True), deps=c.seal())
            ebc = c.emit("vector", lambda v: v.tensor_copy(c.tmpf[:, :], c.ps[7][:, :]), deps=[tb])
            for d in range(4):
                gc = h * 4 + d
                buf = (tc * 16 + gc) % 2
                lo = c.dma("sync", ot_buf[:, buf, :],
                           o_dram.ap()[gc * 262144:(gc + 1) * 262144]
                           .rearrange("(p t) -> p t", p=128)[:, tc * 512:(tc + 1) * 512],
                           deps=[ord_, ("vector", c.cnt["vector"])])
                gbuf = gc % 2
                gwar = ([gw_rd[gbuf]] if gbuf in gw_rd else []) + [("scalar", c.cnt["scalar"])]
                lg = c.dma("gpsimd", wgb[:, gbuf], WgT[:, gc * 128:(gc + 1) * 128].rearrange("(c p) h -> p c h", p=128), deps=gwar)
                bg, warg = obk.take()

                def mmgt(t, tc=tc, gbuf=gbuf, bg=bg):
                    last = None
                    for ch in range(DC):
                        last = t.matmul(c.ps[bg][:, :], wgb[:, gbuf, ch], xn[:, ch, tc * 512:(tc + 1) * 512],
                                        start=(ch == 0), stop=(ch == DC - 1))
                    return last
                tmg = c.emit("tensor", mmgt, deps=[lg] + warg)
                gw_rd[gbuf] = tmg
                es = c.emit("scalar", lambda s, bg=bg: s.activation(gt_buf[:, :], c.ps[bg][:, :], AF.Silu), deps=[tmg])
                obk.took(bg, es)
                m1 = c.emit("vector", lambda v, buf=buf: v.tensor_mul(ost[:, :], ot_buf[:, buf, :], gt_buf[:, :]), deps=[es, lo, ebc])
                m2 = c.emit("vector", lambda v, gc=gc, d=d: v.scalar_tensor_tensor(
                    og_all[:, gc, :], ost[:, :], rmsw[:, d:d + 1], c.tmpf[:, :], ALU.mult, ALU.mult))
        dog = c.seal()
        for gc in range(16):
            wb = gc % 2
            war = ([word[wb]] if wb in word else []) + dog
            lw = c.dma("gpsimd", wo_blk[:, wb], WoutT[gc * 128:(gc + 1) * 128, :], deps=war)

            def mmso(t, gc=gc, wb=wb):
                last = None
                for dmc in range(8):
                    last = t.matmul(c.ps[dmc][:, :], wo_blk[:, wb, dmc * 128:(dmc + 1) * 128],
                                    og_all[:, gc, :], start=(gc == 0), stop=(gc == 15))
                return last
            tms = c.emit("tensor", mmso, deps=[lw] + dog + ([("vector", c.cnt["vector"])] if gc == 0 else []))
            word[wb] = tms
        dso = c.seal()
        for dmc in range(8):
            ev = c.emit("vector", lambda v, dmc=dmc: v.tensor_copy(s_st[:, :], c.ps[dmc][:, :]),
                        deps=dso + ([prev_sst] if prev_sst else []))
            prev_sst = c.dma("sync", s_dram.ap()[dmc * 128:(dmc + 1) * 128, tc * 512:(tc + 1) * 512], s_st[:, :], deps=[ev])
        ord_ = prev_sst
    return c.seal()


def _l3_mmo(c, t, bo, v_h, i, A_sb, Sbf, qt_sb):
    last = None
    for d in range(4):
        t.matmul(c.ps[bo][:, d * 128:(d + 1) * 128], v_h[:, i, d * 128:(d + 1) * 128], A_sb[:, :],
                 start=(d == 0), stop=False)
    for d in range(4):
        for a in range(2):
            last = t.matmul(c.ps[bo][:, d * 128:(d + 1) * 128], Sbf[:, a, d * 128:(d + 1) * 128],
                            qt_sb[:, a, :], start=False, stop=(d == 3 and a == 1))
    return last


def l3_ffn(p, xn, We1_ap, be1_ap, We2_ap, be2_ap, gates_sb, y_acc, h_dram):
    c = p.ctx
    HC = 32
    w1b = c.PB.ap()[:, 0:2048].rearrange("p (b c h) -> p b c h", b=2, c=DC)
    h_stage = c.PB.ap()[:, 2048:2560]
    hti = c.PB.ap()[:, 2560:6656].rearrange("p (i t) -> p i t", i=32)
    be2s = c.PB.ap()[:, 6656:7680]
    be1s = c.sbuf("l3_be1", [128, 32], F32)
    w2full = c.PA.ap()[:, 0:16384].rearrange("p (h d) -> p h d", h=16)
    hbk = Banks(c, range(0, 4))
    ybk = Banks(c, [4, 5, 6, 7])
    w1rd = {}
    prev_hst = None
    h_rd = None
    w2_rd = None
    dep0 = c.seal()
    for e in range(9):
        dbe = c.seal()
        lb1 = c.dma("sync", be1s[:, :], be1_ap[e], deps=dbe)
        lb2 = c.dma("gpsimd", be2s[0:1, :], be2_ap[e], deps=dbe)
        for hc in range(HC):
            buf = hc % 2
            war = ([w1rd[buf]] if buf in w1rd else []) + dbe
            lw = c.dma("gpsimd", w1b[:, buf],
                       We1_ap[e, :, hc * 128:(hc + 1) * 128].rearrange("(c p) h -> p c h", p=128), deps=war)
            for tc in range(4):
                b, warb = hbk.take()

                def mmh(t, buf=buf, tc=tc, b=b):
                    last = None
                    for ch in range(DC):
                        last = t.matmul(c.ps[b][:, :], w1b[:, buf, ch], xn[:, ch, tc * 512:(tc + 1) * 512],
                                        start=(ch == 0), stop=(ch == DC - 1))
                    return last
                tm = c.emit("tensor", mmh, deps=[lw] + warb)
                w1rd[buf] = tm
                ev = c.emit("scalar", lambda s, hc=hc, b=b: s.activation(
                    h_stage[:, :], c.ps[b][:, :], AF.Silu, bias=be1s[:, hc:hc + 1]),
                    deps=[tm, lb1] + ([prev_hst] if prev_hst else []))
                prev_hst = c.dma("sync", h_dram.ap()[hc * 262144:(hc + 1) * 262144]
                                 .rearrange("(p t) -> p t", p=128)[:, tc * 512:(tc + 1) * 512],
                                 h_stage[:, :], deps=[ev])
                hbk.took(b, prev_hst)
        dhp = c.seal()
        for half in range(2):
            lw2 = [c.dma("gpsimd", w2full[:, q * 8:(q + 1) * 8, :],
                         We2_ap[e, half * 2048 + q * 1024:half * 2048 + (q + 1) * 1024, :]
                         .rearrange("(h p) d -> p h d", p=128),
                         deps=dhp + ([w2_rd] if w2_rd else [])) for q in range(2)]
            for tt in range(TT):
                lh = c.dma("sync", hti[:, 0:16, :],
                           h_dram.ap().rearrange("(i p t) -> p i t", i=32, p=128)[:, half * 16:(half + 1) * 16, tt * 128:(tt + 1) * 128],
                           deps=dhp + ([h_rd] if h_rd else []))
                b, war = ybk.take()
                b2, war2 = ybk.take()

                def mmy(t, tt=tt, half=half, b=b, b2=b2):
                    last = None
                    for hc in range(16):
                        for ho, bb in ((0, b), (1, b2)):
                            last = t.matmul(c.ps[bb][:, :], hti[:, hc, :], w2full[:, hc, ho * 512:(ho + 1) * 512],
                                            start=(hc == 0), stop=(half == 0 and hc == 15))
                    if half == 0:
                        pass
                    if half == 1:
                        t.matmul(c.ps[b][:, :], c.ones_row_bf[0:1, :], be2s[0:1, 0:512], start=False, stop=True)
                        last = t.matmul(c.ps[b2][:, :], c.ones_row_bf[0:1, :], be2s[0:1, 512:1024], start=False, stop=True)
                    return last
                tm = c.emit("tensor", mmy, deps=lw2 + [lh, lb2] + war + war2)
                h_rd = tm
                w2_rd = tm
                g_ap = gates_sb[:, tt * 8 + e:tt * 8 + e + 1] if e < 8 else None
                for ho, bb in ((0, b), (1, b2)):
                    ya = y_acc[:, tt, ho * 512:(ho + 1) * 512]
                    if e == 0 and half == 0:
                        r = c.emit("vector", lambda v, ya=ya, bb=bb, g_ap=g_ap: v.tensor_scalar(
                            ya, c.ps[bb][:, :], g_ap, None, ALU.mult), deps=[tm])
                    elif g_ap is not None:
                        r = c.emit("vector", lambda v, ya=ya, bb=bb, g_ap=g_ap: v.scalar_tensor_tensor(
                            ya, c.ps[bb][:, :], g_ap, ya, ALU.mult, ALU.add), deps=[tm])
                    else:
                        r = c.emit("vector", lambda v, ya=ya, bb=bb: v.tensor_add(ya, c.ps[bb][:, :], ya), deps=[tm])
                ybk.took(b, r)
                ybk.took(b2, r)
    return c.seal()


def emit_layer3(p, x_in, x_out):
    c = p.ctx
    WqT = p.inp("L3WqT", [1024, 1024], F32, lambda ci, cc: np.ascontiguousarray(ci["L3_attn_Wq"].T))
    WkT = p.inp("L3WkT", [1024, 1024], F32, lambda ci, cc: np.ascontiguousarray(ci["L3_attn_Wk"].T))
    WvT = p.inp("L3WvT", [1024, 2048], F32, lambda ci, cc: np.ascontiguousarray(ci["L3_attn_Wv"].T))
    WgT = p.inp("L3WgT", [1024, 2048], F32, lambda ci, cc: np.ascontiguousarray(ci["L3_attn_Wg"].T))
    Wgk1T = p.inp("L3Wgk1T", [1024, 32], F32, lambda ci, cc: np.ascontiguousarray(ci["L3_attn_Wgk1"].T))
    Wgk2T = p.inp("L3Wgk2T", [32, 1024], F32, lambda ci, cc: np.ascontiguousarray(ci["L3_attn_Wgk2"].T))
    bgk2r = p.inp("L3bgk2r", [1, 1024], F32, lambda ci, cc: ci["L3_attn_bgk2"][None, :])
    WoutT = p.inp("L3WoutT", [2048, 1024], F32, lambda ci, cc: np.ascontiguousarray(ci["L3_attn_Wout"].T))
    rmsw = p.inp("L3rmsw", [128, 4], F32, lambda ci, cc: np.ascontiguousarray(ci["L3_attn_rms_w"].reshape(4, 128).T))
    hflag = p.inp("L3hflag", [128, 1], F32, lambda ci, cc: np.full((128, 1), float(cc % 2), np.float32))
    m4 = p.inp("L3m4", [128, 128], BF16, lambda ci, cc: np.triu(np.ones((128, 128), BF), 0))
    Wr1T = p.inp("L3Wr1T", [1024, 256], F32, lambda ci, cc: ci["L3_ffn_Wr1"])
    br1 = p.inp("L3br1", [128, 2], F32, lambda ci, cc: np.ascontiguousarray(ci["L3_ffn_br1"].reshape(2, 128).T))
    Wr2 = p.inp("L3Wr2", [256, 8], F32, lambda ci, cc: ci["L3_ffn_Wr2"])
    br2 = p.inp("L3br2", [1, 8], F32, lambda ci, cc: ci["L3_ffn_br2"][None, :])
    We1 = p.inp("L3We1", [9, 1024, 4096], F32,
                lambda ci, cc: np.concatenate([ci["L3_ffn_We1"], ci["L3_ffn_Ws1"][None]], 0))
    be1 = p.inp("L3be1", [9, 128, 32], F32,
                lambda ci, cc: np.concatenate([ci["L3_ffn_be1"], ci["L3_ffn_bs1"][None]], 0).reshape(9, 32, 128).transpose(0, 2, 1))
    We2 = p.inp("L3We2", [9, 4096, 1024], F32,
                lambda ci, cc: np.concatenate([ci["L3_ffn_We2"], ci["L3_ffn_Ws2"][None]], 0))
    be2 = p.inp("L3be2", [9, 1, 1024], F32,
                lambda ci, cc: np.concatenate([ci["L3_ffn_be2"], ci["L3_ffn_bs2"][None]], 0)[:, None, :])
    st_in = c.dram("l3_st_in", [524288], F32)
    st_ag = c.dram("l3_st_ag", [2 * 524288], F32)
    v_dram = c.dram("l3_v", [64 * 65536], BF16)
    o_dram = c.dram("l3_o", [16 * 262144], BF16)
    p.l3_o = o_dram
    s_dram = c.dram("l3_s", [1024, 2048], BF16)
    p.l3_s = s_dram
    h_dram = c.dram("l3_h", [32 * 262144], BF16)
    emit_ln(p, x_in)
    l3_router_f32(p, Wr1T, br1, Wr2, br2, c.logits)
    scr = c.PB.ap()[:, 0:4096].bitcast(F32)
    emit_topk2_gates_l3(p, c.logits, c.gates, scr[:, 512:1024])
    l3_attn(p, c.xn, WqT, WkT, WvT, WgT, Wgk1T, Wgk2T, bgk2r, WoutT, rmsw, hflag,
            m4, st_in, st_ag, v_dram, o_dram, s_dram)
    y_acc = c.PF.ap()[:, :].rearrange("p (t d) -> p t d", t=TT)
    l3_ffn(p, c.xn, We1, be1, We2, be2, c.gates, y_acc, h_dram)
    g_tm = c.PA.ap()[:, 0:TT * 1024].rearrange("p (t d) -> p t d", t=TT)
    emit_silu_tm(p, y_acc, g_tm)
    emit_combine(p, x_in, x_out, s_dram, g_tm)
    return c.seal()



_EMITTERS = [None, None, None, None]
_EMITTERS[0] = lambda p, xi, xo: emit_layer0(p, xi, xo)
_EMITTERS[1] = lambda p, xi, xo: emit_layer1(p, xi, xo)
_EMITTERS[2] = lambda p, xi, xo: emit_layer2(p, xi, xo)
_EMITTERS[3] = lambda p, xi, xo: emit_layer3(p, xi, xo)

_PROGS = {}


def _get_layer_prog(L):
    if L in _PROGS:
        return _PROGS[L]
    p = Prog()

    def body(p):
        xin = p.inp("xT", [1024, 2048], F32, lambda ci, cc: ci["xT"])
        xout = p.out("x_out", [1024, 2048], F32)
        p.init_ops()
        _EMITTERS[L](p, xin, xout)
    p.build(body)
    _PROGS[L] = p
    return p


def kernel(**inputs):
    x = np.asarray(inputs["L0_attn_x"])
    shards = []
    for cc in range(NCORE):
        b, h = cc // 2, cc % 2
        shards.append(np.ascontiguousarray(x[b, h * T:(h + 1) * T, :].T))
    for L in range(4):
        p = _get_layer_prog(L)
        percore = []
        for cc in range(NCORE):
            d = dict(inputs)
            d["xT"] = shards[cc]
            percore.append(d)
        res = p.run(percore)
        shards = [np.ascontiguousarray(res.results[cc]["x_out"]) for cc in range(NCORE)]
    out = np.empty((B, S, D), np.float32)
    for cc in range(NCORE):
        b, h = cc // 2, cc % 2
        out[b, h * T:(h + 1) * T, :] = shards[cc].T
    return out
